# version 34
# baseline (speedup 1.0000x reference)
import contextlib
import sys

import numpy as np
import ml_dtypes

sys.path.insert(0, "/opt/trn_rl_repo")
import concourse.bass as bass
import concourse.mybir as mybir
from concourse.bass_utils import run_bass_kernel_spmd

F32 = mybir.dt.float32
BF16 = mybir.dt.bfloat16
U8 = mybir.dt.uint8
AF = mybir.ActivationFunctionType
ALU = mybir.AluOpType
BF = ml_dtypes.bfloat16

B, T, D, DI, DIH, NST, HID = 4, 4096, 2048, 4096, 2048, 16, 5632
NTT = T // 128
NDT = DIH // 128
NKD = D // 128
NCH = 8
TC = T // NCH
MYT = T // 2
NHT = HID // 128
RG = [[0, 1], [2, 3], [4, 5], [6, 7]]


class Sem:
    def __init__(self, ctx, nc, name, step=1):
        self.h = ctx.enter_context(nc.semaphore(name))
        self.n = 0
        self.step = step

    def inc(self, instr, k=None):
        k = self.step if k is None else k
        instr.then_inc(self.h, k)
        self.n += k
        return instr



def wslice_dma(eng, dst_sb, src_ap, idx, width):
    return eng.dma_start(out=dst_sb[:],
                         in_=src_ap[:, idx * width:(idx + 1) * width])

def build(stop="full"):
    nc = bass.Bass("TRN2", target_bir_lowering=False, debug=False)

    ap_x = nc.dram_tensor("x", [T, D], F32, kind="ExternalInput").ap()
    ap_xmy = nc.dram_tensor("xmy", [MYT, D], F32, kind="ExternalInput").ap()
    ap_wip = nc.dram_tensor("wip", [128, 32 * 16 * 128], F32,
                            kind="ExternalInput").ap()
    ap_wxp = nc.dram_tensor("wxp", [128, 16 * 33], F32, kind="ExternalInput").ap()
    ap_wop = nc.dram_tensor("wop", [128, 16 * 32 * 128], F32,
                            kind="ExternalInput").ap()
    ap_convw = nc.dram_tensor("convw", [128, NDT * 4], F32, kind="ExternalInput").ap()
    ap_convb = nc.dram_tensor("convb", [128, NDT], F32, kind="ExternalInput").ap()
    ap_dtw = nc.dram_tensor("dtw", [128, NDT], F32, kind="ExternalInput").ap()
    ap_dtb = nc.dram_tensor("dtb", [128, NDT], F32, kind="ExternalInput").ap()
    ap_alog = nc.dram_tensor("alog", [128, NDT * NST], F32, kind="ExternalInput").ap()
    ap_dpar = nc.dram_tensor("dpar", [128, NDT], F32, kind="ExternalInput").ap()
    ap_nw = nc.dram_tensor("nw", [128, NKD], F32, kind="ExternalInput").ap()
    ap_wg = nc.dram_tensor("wg", [128, NHT * 16 * 128], U8,
                           kind="ExternalInput").ap()
    ap_wu = nc.dram_tensor("wu", [128, NHT * 16 * 128], U8,
                           kind="ExternalInput").ap()
    ap_wd = nc.dram_tensor("wd", [128, 16 * NHT * 128], U8,
                           kind="ExternalInput").ap()
    ap_sg = nc.dram_tensor("sg", [128, NHT], F32, kind="ExternalInput").ap()
    ap_su = nc.dram_tensor("su", [128, NHT], F32, kind="ExternalInput").ap()
    ap_sd = nc.dram_tensor("sd", [128, NKD], F32, kind="ExternalInput").ap()
    ap_idbf = nc.dram_tensor("idbf", [128, 128], BF16, kind="ExternalInput").ap()
    ap_o = nc.dram_tensor("o", [MYT, D], F32, kind="ExternalOutput").ap()

    xc_hbm = nc.dram_tensor("xc_hbm", [DIH, T], BF16).ap()
    z_hbm = nc.dram_tensor("z_hbm", [DIH, T], BF16).ap()
    x1_hbm = nc.dram_tensor("x1_hbm", [MYT, D], F32).ap()
    fin_hbm = nc.dram_tensor("fin_hbm", [D, MYT], BF16).ap()
    ar_in = [nc.dram_tensor(f"ar_in{i}", [33, MYT], F32).ap() for i in range(2)]
    ar_out = [nc.dram_tensor(f"ar_out{i}", [33, MYT], F32).ap()
              for i in range(2)]
    ar_bfB = nc.dram_tensor("ar_bfB", [NCH * NST, TC], BF16).ap()
    ar_bfC = nc.dram_tensor("ar_bfC", [NCH * NST, TC], BF16).ap()
    ag_in = [nc.dram_tensor(f"ag_in{c}", [128, NDT * TC], BF16).ap()
             for c in range(NCH)]
    ag_out = [nc.dram_tensor(f"ag_out{c}", [256, NDT * TC], BF16).ap()
              for c in range(NCH)]

    dbg = {}
    if stop in ("A", "S", "O"):
        dbg["dbg_xc"] = nc.dram_tensor("dbg_xc", [DIH, T], BF16, kind="ExternalOutput").ap()
        dbg["dbg_z"] = nc.dram_tensor("dbg_z", [DIH, T], BF16, kind="ExternalOutput").ap()
        dbg["dbg_xssm"] = nc.dram_tensor("dbg_xssm", [33, T], F32, kind="ExternalOutput").ap()
    if stop in ("S", "O"):
        dbg["dbg_yg"] = nc.dram_tensor("dbg_yg", [DIH, T], BF16, kind="ExternalOutput").ap()
    if stop == "O":
        dbg["dbg_x1"] = nc.dram_tensor("dbg_x1", [MYT, D], F32, kind="ExternalOutput").ap()
        dbg["dbg_fin"] = nc.dram_tensor("dbg_fin", [D, MYT], BF16, kind="ExternalOutput").ap()
    if stop == "F":
        dbg["dbg_gu"] = nc.dram_tensor("dbg_gu", [HID, MYT], BF16, kind="ExternalOutput").ap()
        dbg["dbg_fo"] = nc.dram_tensor("dbg_fo", [D, MYT], BF16, kind="ExternalOutput").ap()

    ctx = contextlib.ExitStack()
    with ctx:
        def sb(name, shape, dt, stack=None):
            return (stack or ctx).enter_context(nc.sbuf_tensor(name, shape, dt))

        nw_sb = sb("nw_sb", [128, NKD], F32)
        convw_sb = sb("convw_sb", [128, NDT * 4], F32)
        convb_sb = sb("convb_sb", [128, NDT], F32)
        dtw_sb = sb("dtw_sb", [128, NDT], F32)
        dtb_sb = sb("dtb_sb", [128, NDT], F32)
        dpar_sb = sb("dpar_sb", [128, NDT], F32)
        a_sb = sb("a_sb", [128, NDT * NST], F32)
        sg_sb = sb("sg_sb", [128, NHT], F32)
        su_sb = sb("su_sb", [128, NHT], F32)
        sd_sb = sb("sd_sb", [128, NKD], F32)
        idbf_sb = sb("idbf_sb", [128, 128], BF16)
        m_all = sb("m_all", [128, NTT], F32)
        r_all = sb("r_all", [128, NTT], F32)
        s1_all = sb("s1_all", [128, NTT], F32)
        s2_all = sb("s2_all", [128, NTT], F32)
        rq_all = sb("rq_all", [128, NTT], F32)
        vt1 = sb("vt1", [128, NTT], F32)
        vt2 = sb("vt2", [128, NTT], F32)
        ost = sb("ost", [128, 16], F32)
        xssm_part = sb("xssm_part", [33, T], F32)
        wxp_sb = sb("wxp_sb", [128, NKD * 33], BF16)
        carry = sb("carry", [128, NDT * NST], F32)

        pmm = [ctx.enter_context(nc.psum_tensor(f"pmm{i}", [128, 512], F32))
               for i in range(5)]
        paux = ctx.enter_context(nc.psum_tensor("paux", [128, 512], F32))
        ptb = [ctx.enter_context(nc.psum_tensor(f"ptb{i}", [128, 1024], BF16))
               for i in range(2)]

        ld = Sem(ctx, nc, "ld", step=16)
        cst = Sem(ctx, nc, "cst", step=16)
        yf = Sem(ctx, nc, "yf", step=16)
        rp = Sem(ctx, nc, "rp", step=16)
        gld = Sem(ctx, nc, "gld", step=16)
        pe = Sem(ctx, nc, "pe")
        dv = Sem(ctx, nc, "dv")
        ac = Sem(ctx, nc, "ac")
        gp = Sem(ctx, nc, "gp")
        po = Sem(ctx, nc, "po")
        cc = Sem(ctx, nc, "cc")
        agS = [Sem(ctx, nc, f"ag{c}") for c in range(NCH)]
        bar = Sem(ctx, nc, "bar")
        st = Sem(ctx, nc, "st", step=16)
        block = ctx.enter_context(nc.Block())

        E = {"sync": block.sync, "tensor": block.tensor, "vector": block.vector,
             "scalar": block.scalar, "gpsimd": block.gpsimd}

        def barrier():
            def _(sync, w=ld.n):
                sync.wait_ge(ld.h, w)
            E["sync"](_)
            def _(g, w=gld.n):
                g.wait_ge(gld.h, w)
            E["gpsimd"](_)
            target = bar.n + 5
            for e in E.values():
                def _(eng, t=target):
                    eng.nop().then_inc(bar.h, 1)
                    eng.wait_ge(bar.h, t)
                e(_)
            bar.n = target

        def _(sync):
            for dst, src in ((nw_sb, ap_nw), (convb_sb, ap_convb),
                             (dtw_sb, ap_dtw), (dtb_sb, ap_dtb),
                             (dpar_sb, ap_dpar), (sg_sb, ap_sg),
                             (su_sb, ap_su), (sd_sb, ap_sd),
                             (convw_sb, ap_convw), (a_sb, ap_alog)):
                cst.inc(sync.dma_start(out=dst[:], in_=src))
            cst.inc(sync.dma_start(out=idbf_sb[:], in_=ap_idbf))
        E["sync"](_)
        consts_cst = cst.n

        def _(g):
            gld.inc(wslice_dma(g, wxp_sb, ap_wxp, 0, 16 * 33))
            gp.inc(g.memset(carry[:], 0.0))
        E["gpsimd"](_)

        def _(s, w=cst.n):
            s.wait_ge(cst.h, w)
            ac.inc(s.activation(a_sb[:], a_sb[:], AF.Exp, bias=0.0, scale=1.0))
        E["scalar"](_)
        def _(v, w=ac.n, w2=gp.n):
            v.wait_ge(ac.h, w)
            v.wait_ge(gp.h, w2)
            dv.inc(v.tensor_scalar_mul(a_sb[:], a_sb[:], -1.0))
        E["vector"](_)
        consts_dv, consts_gld = dv.n, gld.n

        with contextlib.ExitStack() as actx:
            x_tm = [sb(f"x_tm{i}", [128, D], F32, actx) for i in range(2)]
            xscr = sb("xscr", [128, D], F32, actx)
            xn_tm = [sb(f"xn_tm{i}", [128, D], BF16, actx) for i in range(2)]
            xn_mega = sb("xn_mega", [128, NKD * 2048], BF16, actx)
            wbuf = [sb(f"wbuf{i}", [128, D], BF16, actx) for i in range(2)]
            xi_ext = sb("xi_ext", [128, 2048 + 3], BF16, actx)
            xi_tail = sb("xi_tail", [128, NDT * 3], BF16, actx)
            conv_acc = [sb(f"conv_acc{i}", [128, 512], F32, actx) for i in range(2)]
            xc_mega = sb("xc_mega", [128, NDT * 2048], BF16, actx)
            zbuf = [sb(f"zbuf{i}", [128, 512], BF16, actx) for i in range(2)]

            def _(v):
                dv.inc(v.memset(xi_tail[:], 0.0))
            E["vector"](_)

            xn_done = {}
            xn_half = [0, 0]
            paux_free = [None]
            evac_grp_dv = []
            mm_done_pe = {}
            bank_free = {}
            conv_dv = {}
            ca_silu_ac = [0, 0]
            z_dma_by_buf = [0, 0]
            xc_dma_by_slot = {}
            mega_mm_end = 0
            mega_xproj_pe = 0
            mega_xc_ld = 0
            ar_cc_half = [0, 0]
            ar_cast_half = [0, 0]

            for mega in range(2):
                for tt in range(16):
                    g_tt = mega * 16 + tt
                    xb = x_tm[g_tt % 2]
                    xnb = xn_tm[g_tt % 2]

                    def _(sync, xb=xb, g_tt=g_tt):
                        if g_tt >= 2:
                            sync.wait_ge(dv.h, xn_done[g_tt - 2])
                        ld.inc(sync.dma_start(
                            out=xb[:], in_=ap_x[g_tt * 128:(g_tt + 1) * 128, :]))
                    E["sync"](_)

                    def _(v, xb=xb, g_tt=g_tt, w=ld.n):
                        v.wait_ge(ld.h, w)
                        c = slice(g_tt, g_tt + 1)
                        v.tensor_scalar(xscr[:], xb[:], 1.0, 0.0, ALU.mult, ALU.add,
                                        accum_out=s1_all[:, c])
                        v.scalar_tensor_tensor(xscr[:], xb[:], 1.0, xb[:],
                                               ALU.mult, ALU.mult,
                                               accum_out=s2_all[:, c])
                        v.tensor_scalar_mul(m_all[:, c], s1_all[:, c], 1.0 / D)
                        v.tensor_scalar_mul(vt1[:, c], s2_all[:, c], 1.0 / D)
                        v.tensor_tensor(vt2[:, c], m_all[:, c], m_all[:, c], ALU.mult)
                        v.tensor_tensor(vt1[:, c], vt1[:, c], vt2[:, c], ALU.subtract)
                        v.tensor_scalar(vt2[:, c], vt1[:, c], 1.0 + 1e-5, 1e-10,
                                        ALU.mult, ALU.add)
                        dv.inc(v.tensor_scalar(vt1[:, c], vt1[:, c], 1.0, 1e-5,
                                               ALU.mult, ALU.add))
                    E["vector"](_)

                    def _(s, g_tt=g_tt, w=dv.n):
                        c = slice(g_tt, g_tt + 1)
                        s.wait_ge(dv.h, w)
                        s.activation(r_all[:, c], vt1[:, c], AF.Sqrt, bias=0.0,
                                     scale=1.0)
                        ac.inc(s.activation(rq_all[:, c], vt2[:, c], AF.Sqrt,
                                            bias=0.0, scale=1.0))
                    E["scalar"](_)

                    def _(v, xb=xb, xnb=xnb, g_tt=g_tt, w=ac.n):
                        c = slice(g_tt, g_tt + 1)
                        v.wait_ge(ac.h, w)
                        v.reciprocal(r_all[:, c], r_all[:, c])
                        v.reciprocal(rq_all[:, c], rq_all[:, c])
                        dv.inc(v.tensor_scalar(xnb[:], xb[:], m_all[:, c],
                                               rq_all[:, c], ALU.subtract, ALU.mult))
                    E["vector"](_)
                    xn_done[g_tt] = dv.n

                    for grp in range(2):
                        pb = ptb[grp]
                        def _(t, pb=pb, xnb=xnb, grp=grp, w=dv.n,
                              gi=len(evac_grp_dv), wc=consts_cst):
                            t.wait_ge(dv.h, w)
                            if gi >= 2:
                                t.wait_ge(dv.h, evac_grp_dv[gi - 2])
                            for blk in range(8):
                                db = grp * 8 + blk
                                ins = t.transpose(
                                    pb[:, blk * 128:(blk + 1) * 128],
                                    xnb[:, db * 128:(db + 1) * 128], idbf_sb[:])
                            pe.inc(ins)
                        E["tensor"](_)

                        def _(v, pb=pb, grp=grp, tt=tt, w=pe.n, wp=mega_mm_end):
                            v.wait_ge(pe.h, w)
                            if wp:
                                v.wait_ge(pe.h, wp)
                            src = pb[:].rearrange("p (b c) -> p b c", b=8)
                            dst = xn_mega[:].rearrange("p (dt c) -> p dt c", dt=NKD)[
                                :, grp * 8:(grp + 1) * 8, tt * 128:(tt + 1) * 128]
                            dv.inc(v.tensor_copy(dst, src))
                        E["vector"](_)
                        evac_grp_dv.append(dv.n)
                    if tt in (7, 15):
                        hh = 0 if tt == 7 else 1
                        def _(v, hh=hh, w2=consts_dv):
                            if hh == 0:
                                v.wait_ge(dv.h, w2)
                            xnm = xn_mega[:].rearrange("p (dt c) -> p dt c",
                                                       dt=NKD)
                            for db in range(NKD):
                                ins = v.tensor_scalar_mul(
                                    xnm[:, db, hh * 1024:(hh + 1) * 1024],
                                    xnm[:, db, hh * 1024:(hh + 1) * 1024],
                                    nw_sb[:, db:db + 1])
                            dv.inc(ins)
                        E["vector"](_)
                        xn_half[hh] = dv.n


                for mt in range(32):
                    gmt = mega * 32 + mt
                    wb = wbuf[mt % 2]
                    def _(g, wb=wb, mt=mt, gmt=gmt):
                        if gmt >= 2:
                            g.wait_ge(pe.h, mm_done_pe[gmt - 2])
                        gld.inc(wslice_dma(g, wb, ap_wip, mt, 2048))
                    E["gpsimd"](_)
                    w_ld = gld.n

                    tail_dv = None
                    if mt < 16:
                        def _(v, mt=mt):
                            if mt >= 1:
                                v.wait_ge(dv.h, conv_dv[mt - 1])
                            dv.inc(v.tensor_copy(
                                xi_ext[:, 0:3], xi_tail[:, mt * 3:(mt + 1) * 3]))
                        E["vector"](_)
                        tail_dv = dv.n

                    bids = [(mt * 4 + i) % 5 for i in range(4)]
                    banks4 = [pmm[i] for i in bids]
                    mm_pe_half = [0, 0]
                    for hh in range(2):
                        def _(t, wb=wb, hh=hh, w=w_ld, w2=xn_half[hh],
                              banks4=banks4,
                              wb_=[bank_free.get(i) for i in bids[hh * 2:
                                                                 hh * 2 + 2]]):
                            t.wait_ge(gld.h, w)
                            t.wait_ge(dv.h, w2)
                            for i in range(2):
                                if wb_[i]:
                                    t.wait_ge(wb_[i][0].h, wb_[i][1])
                            for kt in range(NKD):
                                for nch in (hh * 2, hh * 2 + 1):
                                    ins = t.matmul(
                                        banks4[nch][:, 0:512],
                                        wb[:, kt * 128:(kt + 1) * 128],
                                        xn_mega[:, kt * 2048 + nch * 512:
                                                kt * 2048 + nch * 512 + 512],
                                        start=(kt == 0), stop=(kt == NKD - 1))
                            pe.inc(ins)
                        E["tensor"](_)
                        mm_pe_half[hh] = pe.n

                    for nch in range(4):
                        bank_i = bids[nch]
                        bank = banks4[nch]
                        mm_pe = mm_pe_half[nch // 2]

                        if mt < 16:
                            ca = conv_acc[nch % 2]
                            def _(v, bank=bank, ca=ca, mt=mt, nch=nch, w=mm_pe,
                                  wc=ca_silu_ac[nch % 2]):
                                v.wait_ge(pe.h, w)
                                v.tensor_copy(
                                    xi_ext[:, 3 + nch * 512:3 + nch * 512 + 512],
                                    bank[:, 0:512])
                                if wc:
                                    v.wait_ge(ac.h, wc)
                                dv.inc(v.tensor_scalar_mul(
                                    ca[:], xi_ext[:, nch * 512:nch * 512 + 512],
                                    convw_sb[:, mt * 4:mt * 4 + 1]))
                            E["vector"](_)
                            bank_free[bank_i] = (dv, dv.n)
                            def _(v, ca=ca, mt=mt, nch=nch):
                                for k in range(1, 4):
                                    ins = v.scalar_tensor_tensor(
                                        ca[:],
                                        xi_ext[:, nch * 512 + k:nch * 512 + k + 512],
                                        convw_sb[:, mt * 4 + k:mt * 4 + k + 1],
                                        ca[:], ALU.mult, ALU.add)
                                dv.inc(ins)
                            E["vector"](_)
                            def _(s, ca=ca, mt=mt, nch=nch, w=dv.n, mega=mega,
                                  wl=xc_dma_by_slot.get((mt, nch), 0),
                                  wp=mega_xproj_pe):
                                s.wait_ge(dv.h, w)
                                if mega == 1 and wl:
                                    s.wait_ge(ld.h, wl)
                                if mega == 1 and wp:
                                    s.wait_ge(pe.h, wp)
                                ac.inc(s.activation(
                                    xc_mega[:, mt * 2048 + nch * 512:
                                            mt * 2048 + nch * 512 + 512],
                                    ca[:], AF.Silu,
                                    bias=convb_sb[:, mt:mt + 1], scale=1.0))
                            E["scalar"](_)
                            ca_silu_ac[nch % 2] = ac.n
                            def _(sync, mt=mt, nch=nch, mega=mega, w=ac.n):
                                sync.wait_ge(ac.h, w)
                                col = mega * 2048 + nch * 512
                                ld.inc(sync.dma_start(
                                    out=xc_hbm[mt * 128:(mt + 1) * 128, col:col + 512],
                                    in_=xc_mega[:, mt * 2048 + nch * 512:
                                                mt * 2048 + nch * 512 + 512]))
                            E["sync"](_)
                            xc_dma_by_slot[(mt, nch)] = ld.n
                        else:
                            zb = zbuf[nch % 2]
                            def _(v, bank=bank, zb=zb, w=mm_pe,
                                  wl=z_dma_by_buf[nch % 2]):
                                v.wait_ge(pe.h, w)
                                if wl:
                                    v.wait_ge(ld.h, wl)
                                dv.inc(v.tensor_copy(zb[:], bank[:, 0:512]))
                            E["vector"](_)
                            bank_free[bank_i] = (dv, dv.n)
                            def _(sync, zb=zb, mt=mt, nch=nch, mega=mega, w=dv.n):
                                sync.wait_ge(dv.h, w)
                                col = mega * 2048 + nch * 512
                                ld.inc(sync.dma_start(
                                    out=z_hbm[(mt - 16) * 128:(mt - 15) * 128,
                                              col:col + 512],
                                    in_=zb[:]))
                            E["sync"](_)
                            z_dma_by_buf[nch % 2] = ld.n

                    mm_done_pe[gmt] = pe.n

                    if mt < 16:
                        def _(v, mt=mt):
                            dv.inc(v.tensor_copy(
                                xi_tail[:, mt * 3:(mt + 1) * 3],
                                xi_ext[:, 2048:2051]))
                        E["vector"](_)
                        conv_dv[mt] = dv.n

                    if mt == 15:
                        for nch in range(4):
                            def _(t, nch=nch, w=ac.n, wg=consts_gld,
                                  wb_=paux_free[0]):
                                t.wait_ge(ac.h, w)
                                t.wait_ge(gld.h, wg)
                                if wb_:
                                    t.wait_ge(wb_[0].h, wb_[1])
                                for kt in range(NDT):
                                    ins = t.matmul(
                                        paux[0:33, 0:512],
                                        wxp_sb[:, kt * 33:(kt + 1) * 33],
                                        xc_mega[:, kt * 2048 + nch * 512:
                                                kt * 2048 + nch * 512 + 512],
                                        start=(kt == 0), stop=(kt == NDT - 1))
                                pe.inc(ins)
                            E["tensor"](_)
                            def _(v, nch=nch, mega=mega, w=pe.n):
                                v.wait_ge(pe.h, w)
                                col = mega * 2048 + nch * 512
                                dv.inc(v.tensor_copy(
                                    xssm_part[:, col:col + 512],
                                    paux[0:33, 0:512]))
                            E["vector"](_)
                            paux_free[0] = (dv, dv.n)
                        mega_xproj_pe = pe.n

                        def _(g, mega=mega, w=dv.n):
                            g.wait_ge(dv.h, w)
                            gld.inc(g.dma_start(
                                out=ar_in[mega],
                                in_=xssm_part[:, mega * MYT:(mega + 1) * MYT]))
                        E["gpsimd"](_)
                        def _(g, mega=mega, w=gld.n):
                            g.wait_ge(gld.h, w)
                            cc.inc(g.collective_compute(
                                "AllReduce", ALU.add, replica_groups=RG,
                                ins=[ar_in[mega].opt()],
                                outs=[ar_out[mega].opt()]))
                        E["gpsimd"](_)
                        ar_cc_half[mega] = cc.n
                        def _(g, mega=mega, w=cc.n):
                            g.wait_ge(cc.h, w)
                            gld.inc(g.dma_start(
                                out=ar_bfB[mega * 4 * NST:(mega + 1) * 4 * NST]
                                .rearrange("(ch n) c -> n ch c", n=NST),
                                in_=ar_out[mega][0:NST, :]
                                .rearrange("n (ch c) -> n ch c", ch=4)))
                            gld.inc(g.dma_start(
                                out=ar_bfC[mega * 4 * NST:(mega + 1) * 4 * NST]
                                .rearrange("(ch n) c -> n ch c", n=NST),
                                in_=ar_out[mega][NST:2 * NST, :]
                                .rearrange("n (ch c) -> n ch c", ch=4)))
                        E["gpsimd"](_)
                        ar_cast_half[mega] = gld.n

                mega_mm_end = pe.n

            if stop == "A":
                def _(g, w=ld.n):
                    g.wait_ge(ld.h, w)
                    gld.inc(g.dma_start(out=dbg["dbg_xc"], in_=xc_hbm))
                    gld.inc(g.dma_start(out=dbg["dbg_z"], in_=z_hbm))
                E["gpsimd"](_)
                def _(sync, w=dv.n):
                    sync.wait_ge(dv.h, w)
                    ld.inc(sync.dma_start(out=dbg["dbg_xssm"], in_=xssm_part[:]))
                E["sync"](_)
                def _(g, w1=gld.n, w2=ld.n):
                    g.wait_ge(gld.h, w1)
                    g.wait_ge(ld.h, w2)
                E["gpsimd"](_)
                return nc

            barrier()

        with contextlib.ExitStack() as sctx:
            SCW = NST * 513
            xcs = sb("xcs", [128, NDT * TC], BF16, sctx)
            zsb = sb("zsb", [128, NDT * TC], BF16, sctx)
            srep2 = [sb(f"srep{i}", [128, TC], F32, sctx) for i in range(2)]
            brep2 = [sb(f"brep{i}", [128, NST * TC], BF16, sctx)
                     for i in range(2)]
            crep1 = sb("crep", [128, NST * TC], BF16, sctx)
            crep2 = [crep1, crep1]
            w2b = [sb(f"w2b{i}", [128, TC], F32, sctx) for i in range(2)]
            w3b = [sb(f"w3b{i}", [128, TC], F32, sctx) for i in range(2)]
            dt3 = [sb(f"dt3_{i}", [128, TC], F32, sctx) for i in range(3)]
            xt3 = [sb(f"xt3_{i}", [128, TC], BF16, sctx) for i in range(3)]
            ydu = sb("ydu", [128, TC], BF16, sctx)
            da_all = [sb(f"da_all{i}", [128, SCW], BF16, sctx) for i in range(2)]
            xb_all = [sb(f"xb_all{i}", [128, SCW], BF16, sctx) for i in range(2)]
            yacc = sb("yacc", [128, TC], F32, sctx)
            fs2 = sb("fs2", [128, 4 * TC], BF16, sctx)
            fs3 = sb("fs3", [128, 2 * TC], BF16, sctx)

            yg_t = [sb(f"yg_t{i}", [128, TC], BF16, sctx) for i in range(2)]

            def _(v):
                v.memset(da_all[0][:], 0.0)
                dv.inc(v.memset(da_all[1][:], 0.0))
            E["vector"](_)

            rep_free_dv = [0, 0]
            load_used_po = {}
            ygbuf_free_ld = [0, 0]
            zs_free_po2 = 0
            da_free = [0, 0]
            fs_free_po = [0]
            tail_po = {}
            qld_of = {}
            silu_q = {}
            dtxt_free = [0, 0]
            sc_run = {}
            vdt_dv = {}
            v3_of = {}

            def sc_da_wait(ch, dtile, da_free, v3_of):
                return da_free[(dtile - 1) % 2] if dtile >= 1 else 0

            def emit_v3(ch, t):
                pp = t % 2
                dab, xbb = da_all[pp], xb_all[pp]
                def _(v, ch=ch, t=t, pp=pp, dab=dab, xbb=xbb,
                      w=sc_run[(ch, t + 1)],
                      wf=fs_free_po[0]):
                    v.wait_ge(ac.h, w)
                    if wf:
                        v.wait_ge(po.h, wf)
                    xbv = xbb[:].rearrange("p (n c) -> p n c", c=513)
                    hv = xbv
                    v.tensor_tensor(
                        xbv[:, :, 1:513],
                        xt3[t % 3][:].unsqueeze(1).broadcast_to([128, NST, TC]),
                        brep[:].rearrange("p (n c) -> p n c", n=NST),
                        ALU.mult)
                    v.tensor_tensor_scan(xbb[:], dab[:], xbb[:],
                                         0.0, ALU.mult, ALU.add)
                    v.tensor_copy(
                        carry[:, t * NST:(t + 1) * NST]
                        .rearrange("p (n one) -> p n one", one=1),
                        hv[:, :, 512:513])
                    v.tensor_tensor(
                        dab[:, 0:NST * TC].rearrange("p (n c) -> p n c",
                                                     n=NST),
                        hv[:, :, 1:513],
                        crep[:].rearrange("p (n c) -> p n c", n=NST),
                        ALU.mult)
                    v.tensor_tensor(
                        xbb[:, 0:8 * TC], dab[:, 0:8 * TC],
                        dab[:, 8 * TC:16 * TC], ALU.add)
                    dv.inc(v.tensor_tensor(
                        fs2[:], xbb[:, 0:4 * TC],
                        xbb[:, 4 * TC:8 * TC], ALU.add))
                E["vector"](_)
                v3_of[(ch, t)] = dv.n
                da_free[pp] = dv.n
                dtxt_free[pp] = dv.n

                ygb = yg_t[pp]
                def _(g, ch=ch, t=t, ygb=ygb, w=dv.n,
                      wz=silu_q[(ch, t // 4)],
                      wl=ygbuf_free_ld[pp]):
                    g.wait_ge(dv.h, w)
                    g.tensor_tensor(fs3[:], fs2[:, 0:2 * TC],
                                    fs2[:, 2 * TC:4 * TC], ALU.add)
                    g.tensor_tensor(yacc[:], fs3[:, 0:TC],
                                    fs3[:, TC:2 * TC], ALU.add)
                    g.tensor_tensor(
                        ydu[:], xcs[:, t * TC:(t + 1) * TC],
                        dpar_sb[:, t:t + 1].broadcast_to([128, TC]),
                        ALU.mult)
                    g.tensor_tensor(yacc[:], yacc[:], ydu[:], ALU.add)
                    if wl:
                        g.wait_ge(ld.h, wl)
                    po.inc(g.tensor_tensor(
                        ygb[:], yacc[:],
                        zsb[:, t * TC:(t + 1) * TC], ALU.mult))
                E["gpsimd"](_)
                fs_free_po[0] = po.n
                tail_po[(ch, t)] = po.n

                def _(sync, ch=ch, t=t, ygb=ygb, w=po.n):
                    sync.wait_ge(po.h, w)
                    ld.inc(sync.dma_start(
                        out=ag_in[ch][:, t * TC:(t + 1) * TC],
                        in_=ygb[:]))
                    if stop in ("S", "O"):
                        ld.inc(sync.dma_start(
                            out=dbg["dbg_yg"][t * 128:(t + 1) * 128,
                                              ch * TC:(ch + 1) * TC],
                            in_=ygb[:]))
                E["sync"](_)
                ygbuf_free_ld[pp] = ld.n

            for ch in range(NCH):
                cols = slice(ch * TC, (ch + 1) * TC)
                xcb = xcs

                half = ch // 4
                cpp = ch % 2
                brep, crep, srep = brep2[cpp], crep2[cpp], srep2[cpp]
                hcols = slice((ch % 4) * TC, (ch % 4 + 1) * TC)
                def _(s, ch=ch, half=half, hcols=hcols, cpp=cpp,
                      w=ar_cast_half[ch // 4], w2=rep_free_dv[cpp],
                      w3=rep_free_dv[1 - cpp]):
                    if ch % 4 == 0:
                        s.wait_ge(gld.h, w)
                    if w2:
                        s.wait_ge(dv.h, w2)
                    rp.inc(s.dma_start(
                        out=brep2[cpp][:],
                        in_=ar_bfB[ch * NST:(ch + 1) * NST, :]
                        .rearrange("n c -> () (n c)")
                        .broadcast_to([128, NST * TC])))
                    rp.inc(s.dma_start(
                        out=srep2[cpp][:],
                        in_=ar_out[half][32:33, hcols].broadcast_to([128, TC])))
                    if w3:
                        s.wait_ge(dv.h, w3)
                    rp.inc(s.dma_start(
                        out=crep2[cpp][:],
                        in_=ar_bfC[ch * NST:(ch + 1) * NST, :]
                        .rearrange("n c -> () (n c)")
                        .broadcast_to([128, NST * TC])))
                E["scalar"](_)
                rep_gld = rp.n

                for q in range(4):
                    def _(sync, ch=ch, cols=cols, q=q,
                          w=tail_po.get((ch - 1, q * 4 + 3), 0)):
                        if w:
                            sync.wait_ge(po.h, w)
                        ld.inc(sync.dma_start(
                            out=xcs[:, q * 4 * TC:(q + 1) * 4 * TC]
                            .rearrange("p (dt c) -> p dt c", dt=4),
                            in_=xc_hbm[:, cols]
                            .rearrange("(dt p) c -> p dt c", p=128)
                            [:, q * 4:(q + 1) * 4, :]))
                        ld.inc(sync.dma_start(
                            out=zsb[:, q * 4 * TC:(q + 1) * 4 * TC]
                            .rearrange("p (dt c) -> p dt c", dt=4),
                            in_=z_hbm[:, cols]
                            .rearrange("(dt p) c -> p dt c", p=128)
                            [:, q * 4:(q + 1) * 4, :]))
                    E["sync"](_)
                    qld_of[(ch, q)] = ld.n
                chunk_ld = ld.n

                for q in range(4):
                    def _(s, ch=ch, q=q, w=qld_of[(ch, q)]):
                        s.wait_ge(ld.h, w)
                        ac.inc(s.activation(
                            zsb[:, q * 4 * TC:(q + 1) * 4 * TC],
                            zsb[:, q * 4 * TC:(q + 1) * 4 * TC],
                            AF.Silu, bias=0.0, scale=1.0))
                    E["scalar"](_)
                    silu_q[(ch, q)] = ac.n
                siluz_ac = ac.n

                def emit_softplus(s, d):
                    qq = d % 2
                    s.activation(w2b[qq][:], srep[:], AF.Abs,
                                 bias=dtb_sb[:, d:d + 1],
                                 scale=dtw_sb[:, d:d + 1])
                    s.activation(w2b[qq][:], w2b[qq][:], AF.Exp, bias=0.0,
                                 scale=-1.0)
                    s.activation(w2b[qq][:], w2b[qq][:], AF.Ln, bias=1.0,
                                 scale=1.0)
                    return s.activation(w3b[qq][:], srep[:], AF.Relu,
                                        bias=dtb_sb[:, d:d + 1],
                                        scale=dtw_sb[:, d:d + 1])

                def emit_vdt(d):
                    def _(v, d=d, p3=d % 3, qq=d % 2, w=ac.n,
                          wl=qld_of[(ch, d // 4)] if d % 4 <= 1 else 0):
                        v.wait_ge(ac.h, w)
                        if wl:
                            v.wait_ge(ld.h, wl)
                        v.tensor_tensor(dt3[p3][:], w3b[qq][:], w2b[qq][:],
                                        ALU.add)
                        dv.inc(v.tensor_tensor(
                            xt3[p3][:], dt3[p3][:],
                            xcb[:, d * TC:(d + 1) * TC], ALU.mult))
                    E["vector"](_)
                    vdt_dv[(ch, d)] = dv.n

                def _(s, ch=ch, w1=rep_gld):
                    s.wait_ge(rp.h, w1)
                    emit_softplus(s, 0)
                    ac.inc(emit_softplus(s, 1))
                E["scalar"](_)
                emit_vdt(0)

                for dtile in range(NDT):
                    def _(s, ch=ch, dtile=dtile,
                          wda=sc_da_wait(ch, dtile, da_free, v3_of),
                          wdt=vdt_dv.get((ch, dtile - 1), 0)):
                        if wda:
                            s.wait_ge(dv.h, wda)
                        if wdt:
                            s.wait_ge(dv.h, wdt)
                        ins = None
                        if dtile >= 1:
                            pd, ppp = dtile - 1, (dtile - 1) % 2
                            s.activation(
                                xb_all[ppp][:].rearrange(
                                    "p (n c) -> p n c", c=513)[:, :, 0:1],
                                carry[:, pd * NST:(pd + 1) * NST]
                                .rearrange("p (n one) -> p n one", one=1),
                                AF.Copy, bias=0.0, scale=1.0)
                            for n in range(NST):
                                ins = s.activation(
                                    da_all[ppp][:, n * 513 + 1:n * 513 + 513],
                                    dt3[pd % 3][:], AF.Exp, bias=0.0,
                                    scale=a_sb[:, pd * NST + n:pd * NST + n + 1])
                        if dtile + 2 < NDT:
                            ins = emit_softplus(s, dtile + 2)
                        ac.inc(ins)
                    E["scalar"](_)
                    sc_run[(ch, dtile)] = ac.n

                    if dtile >= 1:
                        emit_v3(ch, dtile - 1)
                    if dtile + 1 < NDT:
                        emit_vdt(dtile + 1)

                def _(s, ch=ch, wda=sc_da_wait(ch, NDT, da_free, v3_of),
                      wdt=vdt_dv[(ch, NDT - 1)]):
                    if wda:
                        s.wait_ge(dv.h, wda)
                    s.wait_ge(dv.h, wdt)
                    ppp = (NDT - 1) % 2
                    pd = NDT - 1
                    s.activation(
                        xb_all[ppp][:].rearrange(
                            "p (n c) -> p n c", c=513)[:, :, 0:1],
                        carry[:, pd * NST:(pd + 1) * NST]
                        .rearrange("p (n one) -> p n one", one=1),
                        AF.Copy, bias=0.0, scale=1.0)
                    for n in range(NST):
                        ins = s.activation(
                            da_all[ppp][:, n * 513 + 1:n * 513 + 513],
                            dt3[pd % 3][:], AF.Exp, bias=0.0,
                            scale=a_sb[:, (NDT - 1) * NST + n:
                                       (NDT - 1) * NST + n + 1])
                    ac.inc(ins)
                E["scalar"](_)
                sc_run[(ch, NDT)] = ac.n
                emit_v3(ch, NDT - 1)

                rep_free_dv[cpp] = v3_of[(ch, NDT - 1)]
                load_used_po[ch] = fs_free_po[0]
                zs_free_po2 = fs_free_po[0]

                def _(g, ch=ch, w=ld.n):
                    g.wait_ge(ld.h, w)
                    agS[ch].inc(g.collective_compute(
                        "AllGather", ALU.bypass, replica_groups=RG,
                        ins=[ag_in[ch].opt()], outs=[ag_out[ch].opt()]))
                E["gpsimd"](_)

            if stop == "S":
                def _(sync, w=ld.n):
                    sync.wait_ge(ld.h, w)
                E["sync"](_)
                def _(g, w=gld.n):
                    g.wait_ge(gld.h, w)
                    for c in range(NCH):
                        g.wait_ge(agS[c].h, agS[c].n)
                E["gpsimd"](_)
                return nc

            barrier()

        ptb_evac_global = [0, 0]
        with contextlib.ExitStack() as octx:
            yful = [sb(f"yful{i}", [128, 32 * TC], BF16, octx)
                    for i in range(2)]
            wopb = [sb(f"wopb{i}", [128, 32 * 128], BF16, octx) for i in range(2)]
            om_bf = sb("om_bf", [128, 16 * TC], BF16, octx)
            ot_tm = [sb(f"ot_tm{i}", [128, D], F32, octx) for i in range(2)]
            xrb = [sb(f"xrb{i}", [128, D], F32, octx) for i in range(2)]
            axb = sb("axb", [128, D], F32, octx)
            aob = sb("aob", [128, D], F32, octx)
            xscr2 = axb
            x1b = [sb(f"x1b{i}", [128, D], F32, octx) for i in range(2)]
            fnb = [sb(f"fnb{i}", [128, D], BF16, octx) for i in range(2)]
            fstage = sb("fstage", [128, NKD * 256], BF16, octx)

            wop_dma_by_buf = [0, 0]
            yful_used_pe = [0, 0, 0, 0]
            ptb_evac = ptb_evac_global
            x_used_dv = [0, 0]
            x1_used_ld = [0, 0]
            fn_used_ac = [0, 0]
            ot_used_dv = [0, 0]
            fst_dma_ld = [0]

            def load_yful(k):
                yb = yful[k % 2]
                def _(sync, k=k, yb=yb, w=yful_used_pe[k - 2] if k >= 2 else 0):
                    if w:
                        sync.wait_ge(pe.h, w)
                    for arm in nc.switch_core_id(engines=[sync], n=128):
                        dh = arm.logical % 2
                        g = dh * 4 + k
                        sync.wait_ge(agS[g].h, 1)
                        sync.dma_start(
                            out=yb[:].rearrange("p (r f) -> p r f", r=2),
                            in_=ag_out[g].rearrange("(r p) f -> p r f", p=128)
                        ).then_inc(yf.h, 16)
                E["sync"](_)
                yf.n += 16
                return yf.n

            ombf_dv_of = [0] * 4

            def emit_chains(kq):
                    for tt4 in range(4):
                        myrow = kq * 512 + tt4 * 128
                        otb = ot_tm[tt4 % 2]
                        xr = xrb[tt4 % 2]
                        x1t = x1b[tt4 % 2]
                        fnt = fnb[tt4 % 2]

                        def _(sync, xr=xr, myrow=myrow, w=x_used_dv[tt4 % 2]):
                            if w:
                                sync.wait_ge(dv.h, w)
                            ld.inc(sync.dma_start(
                                out=xr[:], in_=ap_xmy[myrow:myrow + 128, :]))
                        E["sync"](_)
                        xr_ld = ld.n

                        for grp in range(2):
                            pb = ptb[grp]
                            def _(t, pb=pb, grp=grp, tt4=tt4, w=ombf_dv_of[kq],
                                  w2=ot_used_dv[tt4 % 2], w3=ptb_evac[grp]):
                                t.wait_ge(dv.h, w)
                                if w2:
                                    t.wait_ge(dv.h, w2)
                                if w3:
                                    t.wait_ge(dv.h, w3)
                                for blk in range(8):
                                    mt = grp * 8 + blk
                                    ins = t.transpose(
                                        pb[:, blk * 128:(blk + 1) * 128],
                                        om_bf[:, mt * TC + tt4 * 128:
                                              mt * TC + tt4 * 128 + 128],
                                        idbf_sb[:])
                                pe.inc(ins)
                            E["tensor"](_)
                            def _(v, pb=pb, grp=grp, otb=otb, w=pe.n):
                                v.wait_ge(pe.h, w)
                                src = pb[:].rearrange("p (b c) -> p b c", b=8)
                                dst = otb[:].rearrange("p (mt c) -> p mt c", mt=16)[
                                    :, grp * 8:(grp + 1) * 8, :]
                                dv.inc(v.tensor_copy(dst, src))
                            E["vector"](_)
                            ptb_evac[grp] = dv.n

                        def _(v, xr=xr, w=xr_ld):
                            v.wait_ge(ld.h, w)
                            v.tensor_scalar(xscr2[:], xr[:], 1.0, 0.0, ALU.mult, ALU.add,
                                            accum_out=ost[:, 0:1])
                            v.scalar_tensor_tensor(xscr2[:], xr[:], 1.0, xr[:],
                                                   ALU.mult, ALU.mult,
                                                   accum_out=ost[:, 1:2])
                            v.tensor_scalar_mul(ost[:, 0:1], ost[:, 0:1], 1.0 / D)
                            v.tensor_scalar_mul(ost[:, 1:2], ost[:, 1:2], 1.0 / D)
                            v.tensor_tensor(ost[:, 2:3], ost[:, 0:1], ost[:, 0:1],
                                            ALU.mult)
                            v.tensor_tensor(ost[:, 1:2], ost[:, 1:2], ost[:, 2:3],
                                            ALU.subtract)
                            dv.inc(v.tensor_scalar(ost[:, 1:2], ost[:, 1:2], 1.0,
                                                   1e-5, ALU.mult, ALU.add))
                        E["vector"](_)
                        def _(s, w=dv.n):
                            s.wait_ge(dv.h, w)
                            ac.inc(s.activation(ost[:, 1:2], ost[:, 1:2], AF.Sqrt,
                                                bias=0.0, scale=1.0))
                        E["scalar"](_)
                        def _(v, w=ac.n):
                            v.wait_ge(ac.h, w)
                            dv.inc(v.reciprocal(ost[:, 1:2], ost[:, 1:2]))
                        E["vector"](_)
                        mr_ac = ac.n
                        mr_dv = dv.n

                        def _(v, xr=xr, otb=otb, x1t=x1t, w=mr_ac, w2=ac.n):
                            v.wait_ge(ac.h, w2)
                            v.tensor_scalar(axb[:], xr[:], ost[:, 0:1], ost[:, 1:2],
                                            ALU.subtract, ALU.mult)
                            v.tensor_tensor(aob[:], otb[:], axb[:], ALU.add)
                            v.tensor_scalar(xscr2[:], aob[:], 1.0, 0.0, ALU.mult, ALU.add,
                                            accum_out=ost[:, 4:5])
                            v.scalar_tensor_tensor(xscr2[:], aob[:], 1.0, aob[:],
                                                   ALU.mult, ALU.mult,
                                                   accum_out=ost[:, 5:6])
                            v.tensor_scalar_mul(ost[:, 4:5], ost[:, 4:5], 1.0 / D)
                            v.tensor_scalar_mul(ost[:, 5:6], ost[:, 5:6], 1.0 / D)
                            v.tensor_tensor(ost[:, 6:7], ost[:, 4:5], ost[:, 4:5],
                                            ALU.mult)
                            v.tensor_tensor(ost[:, 5:6], ost[:, 5:6], ost[:, 6:7],
                                            ALU.subtract)
                            dv.inc(v.tensor_scalar(ost[:, 5:6], ost[:, 5:6], 1.0,
                                                   1e-5, ALU.mult, ALU.add))
                        E["vector"](_)
                        def _(s, w=dv.n):
                            s.wait_ge(dv.h, w)
                            ac.inc(s.activation(ost[:, 5:6], ost[:, 5:6], AF.Sqrt,
                                                bias=0.0, scale=1.0))
                        E["scalar"](_)
                        def _(v, w=ac.n):
                            v.wait_ge(ac.h, w)
                            dv.inc(v.reciprocal(ost[:, 5:6], ost[:, 5:6]))
                        E["vector"](_)

                        def _(v, xr=xr, x1t=x1t, w=ac.n, wl=x1_used_ld[tt4 % 2]):
                            v.wait_ge(ac.h, w)
                            if wl:
                                v.wait_ge(ld.h, wl)
                            v.tensor_scalar(xscr2[:], aob[:], ost[:, 4:5],
                                            ost[:, 5:6], ALU.subtract, ALU.mult)
                            v.tensor_tensor(x1t[:], xscr2[:], xr[:], ALU.add)
                            v.tensor_scalar(xscr2[:], x1t[:], 1.0, 0.0, ALU.mult, ALU.add,
                                            accum_out=ost[:, 8:9])
                            v.scalar_tensor_tensor(xscr2[:], x1t[:], 1.0, x1t[:],
                                                   ALU.mult, ALU.mult,
                                                   accum_out=ost[:, 9:10])
                            v.tensor_scalar_mul(ost[:, 8:9], ost[:, 8:9], 1.0 / D)
                            v.tensor_scalar_mul(ost[:, 9:10], ost[:, 9:10], 1.0 / D)
                            v.tensor_tensor(ost[:, 10:11], ost[:, 8:9], ost[:, 8:9],
                                            ALU.mult)
                            v.tensor_tensor(ost[:, 9:10], ost[:, 9:10],
                                            ost[:, 10:11], ALU.subtract)
                            dv.inc(v.tensor_scalar(ost[:, 9:10], ost[:, 9:10], 1.0,
                                                   1e-5, ALU.mult, ALU.add))
                        E["vector"](_)
                        x_used_dv[tt4 % 2] = dv.n
                        def _(s, w=dv.n):
                            s.wait_ge(dv.h, w)
                            ac.inc(s.activation(ost[:, 9:10], ost[:, 9:10], AF.Sqrt,
                                                bias=0.0, scale=1.0))
                        E["scalar"](_)
                        def _(v, x1t=x1t, fnt=fnt, w=ac.n, wf=fn_used_ac[tt4 % 2]):
                            v.wait_ge(ac.h, w)
                            v.reciprocal(ost[:, 9:10], ost[:, 9:10])
                            dv.inc(v.tensor_scalar(fnt[:], x1t[:], ost[:, 8:9],
                                                   ost[:, 9:10], ALU.subtract,
                                                   ALU.mult))
                        E["vector"](_)
                        fn_dv = dv.n

                        def _(sync, x1t=x1t, myrow=myrow, w=fn_dv):
                            sync.wait_ge(dv.h, w)
                            ld.inc(sync.dma_start(
                                out=x1_hbm[myrow:myrow + 128, :], in_=x1t[:]))
                        E["sync"](_)
                        x1_used_ld[tt4 % 2] = ld.n

                        for grp in range(2):
                            pb = ptb[grp]
                            def _(t, pb=pb, fnt=fnt, grp=grp, w=fn_dv,
                                  wf=fst_dma_ld[0], w3=ptb_evac[grp]):
                                t.wait_ge(dv.h, w)
                                if wf:
                                    t.wait_ge(ld.h, wf)
                                if w3:
                                    t.wait_ge(dv.h, w3)
                                for blk in range(8):
                                    db = grp * 8 + blk
                                    ins = t.transpose(
                                        pb[:, blk * 128:(blk + 1) * 128],
                                        fnt[:, db * 128:(db + 1) * 128], idbf_sb[:])
                                pe.inc(ins)
                            E["tensor"](_)
                            def _(v, pb=pb, grp=grp, tt4=tt4, w=pe.n):
                                v.wait_ge(pe.h, w)
                                src = pb[:].rearrange("p (b c) -> p b c", b=8)
                                dst = fstage[:].rearrange(
                                    "p (db c) -> p db c", db=NKD)[
                                    :, grp * 8:(grp + 1) * 8,
                                    (tt4 % 2) * 128:(tt4 % 2) * 128 + 128]
                                dv.inc(v.tensor_copy(dst, src))
                            E["vector"](_)
                            ptb_evac[grp] = dv.n
                        fn_used_ac[tt4 % 2] = dv.n
                        ot_used_dv[tt4 % 2] = dv.n

                        if tt4 % 2 == 1:
                            def _(sync, kq=kq, tt4=tt4, w=dv.n):
                                sync.wait_ge(dv.h, w)
                                for db in range(NKD):
                                    ins = sync.dma_start(
                                        out=fin_hbm[db * 128:(db + 1) * 128,
                                                    kq * 512 + (tt4 - 1) * 128:
                                                    kq * 512 + (tt4 + 1) * 128],
                                        in_=fstage[:, db * 256:(db + 1) * 256])
                                    ld.inc(ins)
                            E["sync"](_)
                            fst_dma_ld[0] = ld.n



            yful_ld_of = [0] * 4
            yful_ld_of[0] = load_yful(0)
            yful_ld_of[1] = load_yful(1)
            for k in range(4):
                if k >= 1:
                    emit_chains(k - 1)
                yb = yful[k % 2]
                yful_ld = yful_ld_of[k]

                for mt in range(16):
                    wob = wopb[mt % 2]
                    def _(g, mt=mt, wob=wob, w=wop_dma_by_buf[mt % 2]):
                        if w:
                            g.wait_ge(pe.h, w)
                        gld.inc(wslice_dma(g, wob, ap_wop, mt, 4096))
                    E["gpsimd"](_)
                    w_gld = gld.n

                    bank_i = mt % 5
                    bank = pmm[bank_i]
                    def _(t, wob=wob, yb=yb, bank=bank, bank_i=bank_i,
                          w=w_gld, w2=yful_ld,
                          wb_=bank_free.get(f"o{bank_i}")):
                        t.wait_ge(gld.h, w)
                        t.wait_ge(yf.h, w2)
                        if wb_:
                            t.wait_ge(wb_[0].h, wb_[1])
                        for kt in range(32):
                            ins = t.matmul(
                                bank[:, 0:512],
                                wob[:, kt * 128:(kt + 1) * 128],
                                yb[:, kt * TC:(kt + 1) * TC],
                                start=(kt == 0), stop=(kt == 31))
                        pe.inc(ins)
                    E["tensor"](_)
                    wop_dma_by_buf[mt % 2] = pe.n

                    def _(v, bank=bank, mt=mt, w=pe.n):
                        v.wait_ge(pe.h, w)
                        dv.inc(v.tensor_copy(om_bf[:, mt * TC:(mt + 1) * TC],
                                             bank[:, 0:512]))
                    E["vector"](_)
                    bank_free[f"o{bank_i}"] = (dv, dv.n)
                yful_used_pe[k] = pe.n
                if k + 2 < 4:
                    yful_ld_of[k + 2] = load_yful(k + 2)
                ombf_dv = dv.n

                ombf_dv_of[k] = dv.n

            emit_chains(3)

            if stop == "O":
                def _(sync, w=ld.n):
                    sync.wait_ge(ld.h, w)
                    ld.inc(sync.dma_start(out=dbg["dbg_x1"], in_=x1_hbm))
                    ld.inc(sync.dma_start(out=dbg["dbg_fin"], in_=fin_hbm))
                    sync.wait_ge(ld.h, ld.n)
                E["sync"](_)
                def _(g, w=gld.n):
                    g.wait_ge(gld.h, w)
                E["gpsimd"](_)
                return nc

            barrier()

        with contextlib.ExitStack() as fctx:
            QW = 512
            finb = sb("finb", [128, NKD * QW], BF16, fctx)
            wgb = [sb(f"wgb{i}", [128, NKD * 128], BF16, fctx) for i in range(3)]
            wub = [sb(f"wub{i}", [128, NKD * 128], BF16, fctx) for i in range(3)]
            wdb = [sb(f"wdb{i}", [128, NHT * 128], BF16, fctx) for i in range(3)]
            gbuf2 = [sb(f"gbuf{i}", [128, QW], BF16, fctx) for i in range(2)]
            ubuf2 = [sb(f"ubuf{i}", [128, QW], BF16, fctx) for i in range(2)]
            gu_sb = sb("gu_sb", [128, NHT * QW], BF16, fctx)
            fo_bf = sb("fo_bf", [128, NKD * QW], BF16, fctx)
            fo_tm = [sb(f"fo_tm{i}", [128, D], F32, fctx) for i in range(2)]
            x1r = [sb(f"x1r{i}", [128, D], F32, fctx) for i in range(2)]
            fscr = sb("fscr", [128, D], F32, fctx)

            wq_dma_by_buf = {}
            fin_used_pe = 0
            gu_used_pe = 0
            x1r_used_dv = [0, 0]
            x2_st_by_buf = [0, 0]
            fo_used_dv = [0, 0]

            for hf in range(4):
                hcols = slice(hf * QW, (hf + 1) * QW)
                def _(sync, hcols=hcols, w=fin_used_pe):
                    if w:
                        sync.wait_ge(pe.h, w)
                    ld.inc(sync.dma_start(
                        out=finb[:].rearrange("p (dt c) -> p dt c", dt=NKD),
                        in_=fin_hbm[:, hcols].rearrange("(dt p) c -> p dt c", p=128)))
                E["sync"](_)
                fin_ld = ld.n

                gu_par_dv = [0, 0]
                gu_dma_gld = {}
                gu_sub_dv = {}

                def emit_gu_dma(ht):
                    def _(g, ht=ht, wgt=wgb[ht % 3], wut=wub[ht % 3],
                          w=wq_dma_by_buf.get(ht % 3, 0)):
                        if w:
                            g.wait_ge(pe.h, w)
                        gld.inc(wslice_dma(g, wgt, ap_wg, ht, 2048))
                        gld.inc(wslice_dma(g, wut, ap_wu, ht, 2048))
                    E["gpsimd"](_)
                    gu_dma_gld[ht] = gld.n

                def emit_gu_sub(ht):
                    def _(v, wgt=wgb[ht % 3], wut=wub[ht % 3],
                          w=gu_dma_gld[ht], wp=gu_used_pe):
                        v.wait_ge(gld.h, w)
                        if wp:
                            v.wait_ge(pe.h, wp)
                        v.tensor_scalar(wgt[:], wgt[:], 128.0, None, ALU.subtract)
                        dv.inc(v.tensor_scalar(wut[:], wut[:], 128.0, None,
                                               ALU.subtract))
                    E["vector"](_)
                    gu_sub_dv[ht] = dv.n

                emit_gu_dma(0)
                emit_gu_dma(1)
                emit_gu_sub(0)
                for ht in range(NHT):
                    wgt = wgb[ht % 3]
                    wut = wub[ht % 3]
                    if ht + 2 < NHT:
                        emit_gu_dma(ht + 2)
                    if ht + 1 < NHT:
                        emit_gu_sub(ht + 1)
                    wsub_dv = gu_sub_dv[ht]

                    for which, wt in enumerate((wgt, wut)):
                        bank_i = (ht * 2 + which) % 5
                        bank = pmm[bank_i]
                        def _(t, wt=wt, bank=bank, bank_i=bank_i,
                              w=wsub_dv, w2=fin_ld,
                              wb_=bank_free.get(f"f{bank_i}")):
                            t.wait_ge(dv.h, w)
                            t.wait_ge(ld.h, w2)
                            if wb_:
                                t.wait_ge(wb_[0].h, wb_[1])
                            for kt in range(NKD):
                                ins = t.matmul(
                                    bank[:, 0:QW],
                                    wt[:, kt * 128:(kt + 1) * 128],
                                    finb[:, kt * QW:(kt + 1) * QW],
                                    start=(kt == 0), stop=(kt == NKD - 1))
                            pe.inc(ins)
                        E["tensor"](_)
                        if which == 0:
                            def _(s, bank=bank, ht=ht, w=pe.n,
                                  wg_=gu_par_dv[ht % 2]):
                                s.wait_ge(pe.h, w)
                                if wg_:
                                    s.wait_ge(dv.h, wg_)
                                ac.inc(s.activation(
                                    gbuf2[ht % 2][:], bank[:, 0:QW], AF.Silu,
                                    bias=0.0, scale=sg_sb[:, ht:ht + 1]))
                            E["scalar"](_)
                            bank_free[f"f{bank_i}"] = (ac, ac.n)
                        else:
                            def _(v, bank=bank, ht=ht, w=pe.n):
                                v.wait_ge(pe.h, w)
                                dv.inc(v.tensor_scalar_mul(
                                    ubuf2[ht % 2][:], bank[:, 0:QW],
                                    su_sb[:, ht:ht + 1]))
                            E["vector"](_)
                            bank_free[f"f{bank_i}"] = (dv, dv.n)

                    wq_dma_by_buf[ht % 3] = pe.n
                    def _(v, ht=ht, w=ac.n):
                        v.wait_ge(ac.h, w)
                        dv.inc(v.tensor_tensor(
                            gu_sb[:, ht * QW:(ht + 1) * QW],
                            gbuf2[ht % 2][:], ubuf2[ht % 2][:], ALU.mult))
                    E["vector"](_)
                    gu_par_dv[ht % 2] = dv.n
                gu_dv = dv.n
                fin_used_pe = pe.n

                wd_dma_gld = {}
                wd_sub_dv = {}

                def emit_wd_dma(mt):
                    def _(g, mt=mt, wdt=wdb[mt % 3],
                          w=wq_dma_by_buf.get(f"d{mt % 3}", 0)):
                        if w:
                            g.wait_ge(pe.h, w)
                        gld.inc(wslice_dma(g, wdt, ap_wd, mt, 5632))
                    E["gpsimd"](_)
                    wd_dma_gld[mt] = gld.n

                def emit_wd_sub(mt):
                    def _(v, wdt=wdb[mt % 3], w=wd_dma_gld[mt], wp=gu_used_pe):
                        v.wait_ge(gld.h, w)
                        if wp:
                            v.wait_ge(pe.h, wp)
                        dv.inc(v.tensor_scalar(wdt[:], wdt[:], 128.0, None,
                                               ALU.subtract))
                    E["vector"](_)
                    wd_sub_dv[mt] = dv.n

                emit_wd_dma(0)
                emit_wd_dma(1)
                emit_wd_sub(0)
                for mt in range(16):
                    wdt = wdb[mt % 3]
                    if mt + 2 < 16:
                        emit_wd_dma(mt + 2)
                    if mt + 1 < 16:
                        emit_wd_sub(mt + 1)
                    wd_dv = wd_sub_dv[mt]

                    bank_i = mt % 5
                    bank = pmm[bank_i]
                    def _(t, wdt=wdt, bank=bank, bank_i=bank_i,
                          w=wd_dv, w2=gu_dv,
                          wb_=bank_free.get(f"f{bank_i}")):
                        t.wait_ge(dv.h, w)
                        t.wait_ge(dv.h, w2)
                        if wb_:
                            t.wait_ge(wb_[0].h, wb_[1])
                        for kt in range(NHT):
                            ins = t.matmul(
                                bank[:, 0:QW],
                                wdt[:, kt * 128:(kt + 1) * 128],
                                gu_sb[:, kt * QW:(kt + 1) * QW],
                                start=(kt == 0), stop=(kt == NHT - 1))
                        pe.inc(ins)
                    E["tensor"](_)
                    wq_dma_by_buf[f"d{mt % 3}"] = pe.n
                    def _(v, bank=bank, mt=mt, w=pe.n):
                        v.wait_ge(pe.h, w)
                        dv.inc(v.tensor_scalar_mul(
                            fo_bf[:, mt * QW:(mt + 1) * QW],
                            bank[:, 0:QW], sd_sb[:, mt:mt + 1]))
                    E["vector"](_)
                    bank_free[f"f{bank_i}"] = (dv, dv.n)
                gu_used_pe = pe.n
                fo_dv = dv.n

                if stop == "F":
                    def _(sync, hf=hf, w=gu_dv, w2=fo_dv):
                        sync.wait_ge(dv.h, w)
                        sync.wait_ge(dv.h, w2)
                        for kt in range(NHT):
                            ins = sync.dma_start(
                                out=dbg["dbg_gu"][kt * 128:(kt + 1) * 128,
                                                  hf * QW:(hf + 1) * QW],
                                in_=gu_sb[:, kt * QW:(kt + 1) * QW])
                            ld.inc(ins)
                        for mt in range(16):
                            ins = sync.dma_start(
                                out=dbg["dbg_fo"][mt * 128:(mt + 1) * 128,
                                                  hf * QW:(hf + 1) * QW],
                                in_=fo_bf[:, mt * QW:(mt + 1) * QW])
                            ld.inc(ins)
                    E["sync"](_)

                for tt8 in range(4):
                    myrow = hf * QW + tt8 * 128
                    fot = fo_tm[tt8 % 2]
                    x1t = x1r[tt8 % 2]

                    def _(sync, x1t=x1t, myrow=myrow, w=x1r_used_dv[tt8 % 2],
                          ws=x2_st_by_buf[tt8 % 2]):
                        if w:
                            sync.wait_ge(dv.h, w)
                        if ws:
                            sync.wait_ge(st.h, ws)
                        ld.inc(sync.dma_start(
                            out=x1t[:], in_=x1_hbm[myrow:myrow + 128, :]))
                    E["sync"](_)
                    x1_ld = ld.n

                    for grp in range(2):
                        pb = ptb[grp]
                        def _(t, pb=pb, grp=grp, tt8=tt8, w=fo_dv,
                              w2=fo_used_dv[tt8 % 2],
                              w3=ptb_evac_global[grp]):
                            t.wait_ge(dv.h, w)
                            if w2:
                                t.wait_ge(dv.h, w2)
                            if w3:
                                t.wait_ge(dv.h, w3)
                            for blk in range(8):
                                mt = grp * 8 + blk
                                ins = t.transpose(
                                    pb[:, blk * 128:(blk + 1) * 128],
                                    fo_bf[:, mt * QW + tt8 * 128:
                                          mt * QW + tt8 * 128 + 128],
                                    idbf_sb[:])
                            pe.inc(ins)
                        E["tensor"](_)
                        def _(v, pb=pb, grp=grp, fot=fot, w=pe.n):
                            v.wait_ge(pe.h, w)
                            src = pb[:].rearrange("p (b c) -> p b c", b=8)
                            dst = fot[:].rearrange("p (mt c) -> p mt c", mt=16)[
                                :, grp * 8:(grp + 1) * 8, :]
                            dv.inc(v.tensor_copy(dst, src))
                        E["vector"](_)
                        ptb_evac_global[grp] = dv.n

                    def _(v, fot=fot, x1t=x1t, wl=x1_ld,
                          ws=x2_st_by_buf[tt8 % 2]):
                        v.wait_ge(ld.h, wl)
                        if ws:
                            v.wait_ge(st.h, ws)
                        v.tensor_scalar(fscr[:], fot[:], 1.0, 0.0, ALU.mult,
                                        ALU.add, accum_out=ost[:, 12:13])
                        v.scalar_tensor_tensor(fscr[:], fot[:], 1.0, fot[:],
                                               ALU.mult, ALU.mult,
                                               accum_out=ost[:, 13:14])
                        v.tensor_scalar_mul(ost[:, 12:13], ost[:, 12:13], 1.0 / D)
                        v.tensor_scalar_mul(ost[:, 13:14], ost[:, 13:14], 1.0 / D)
                        v.tensor_tensor(ost[:, 14:15], ost[:, 12:13],
                                        ost[:, 12:13], ALU.mult)
                        v.tensor_tensor(ost[:, 13:14], ost[:, 13:14],
                                        ost[:, 14:15], ALU.subtract)
                        dv.inc(v.tensor_scalar(ost[:, 13:14], ost[:, 13:14],
                                               1.0, 1e-5, ALU.mult, ALU.add))
                    E["vector"](_)
                    def _(s, w=dv.n):
                        s.wait_ge(dv.h, w)
                        ac.inc(s.activation(ost[:, 13:14], ost[:, 13:14],
                                            AF.Sqrt, bias=0.0, scale=1.0))
                    E["scalar"](_)
                    def _(v, fot=fot, x1t=x1t, w=ac.n):
                        v.wait_ge(ac.h, w)
                        v.reciprocal(ost[:, 13:14], ost[:, 13:14])
                        v.tensor_scalar(fscr[:], fot[:], ost[:, 12:13],
                                        ost[:, 13:14], ALU.subtract, ALU.mult)
                        dv.inc(v.tensor_tensor(x1t[:], fscr[:], x1t[:], ALU.add))
                    E["vector"](_)
                    x1r_used_dv[tt8 % 2] = dv.n
                    fo_used_dv[tt8 % 2] = dv.n

                    def _(sync, x1t=x1t, myrow=myrow, w=dv.n):
                        sync.wait_ge(dv.h, w)
                        st.inc(sync.dma_start(
                            out=ap_o[myrow:myrow + 128, :], in_=x1t[:]))
                    E["sync"](_)
                    x2_st_by_buf[tt8 % 2] = st.n

            def _(sync, w=st.n, w2=ld.n):
                sync.wait_ge(st.h, w)
                sync.wait_ge(ld.h, w2)
            E["sync"](_)
            def _(g, w=gld.n):
                g.wait_ge(gld.h, w)
            E["gpsimd"](_)

    return nc


_NC_CACHE = {}


def kernel(**inputs):
    d = {k: np.asarray(v) for k, v in inputs.items()}
    idbf = np.eye(128).astype(BF)

    def pcols(a, width=128):
        a = np.asarray(a).astype(np.float32)
        k = a.shape[0] // width
        rest = int(np.prod(a.shape[1:])) if a.ndim > 1 else 1
        return np.ascontiguousarray(
            a.reshape(k, width, rest).transpose(1, 0, 2).reshape(width, k * rest))

    def wpack(a, nkt, nmt):
        a = np.asarray(a)
        return np.ascontiguousarray(
            a.reshape(nkt, 128, nmt, 128).transpose(1, 2, 0, 3)
            .reshape(128, nmt * nkt * 128))

    wip_full = d["L0_attn_in_proj_w"].astype(np.float32)
    wxp = d["L0_attn_x_proj_w"].astype(np.float32)
    wop = wpack(np.ascontiguousarray(d["L0_attn_out_proj_w"].astype(np.float32).T),
                32, 16)
    wg = wpack(np.ascontiguousarray(d["L0_ffn_wq_gate"].T).astype(np.uint8), 16, NHT)
    wu = wpack(np.ascontiguousarray(d["L0_ffn_wq_up"].T).astype(np.uint8), 16, NHT)
    wd = wpack(np.ascontiguousarray(d["L0_ffn_wq_down"].T).astype(np.uint8), NHT, 16)
    x = d["L0_attn_x"].astype(np.float32)

    maps = []
    for cid in range(8):
        b, dh = cid // 2, cid % 2
        sl = slice(dh * DIH, (dh + 1) * DIH)
        wip = wpack(np.ascontiguousarray(
            np.concatenate([wip_full[sl],
                            wip_full[DI + dh * DIH:DI + (dh + 1) * DIH]], 0).T),
            16, 32)
        wxpc = np.ascontiguousarray(wxp[:, sl].T)
        wxpc = np.ascontiguousarray(
            wxpc.reshape(16, 128, 33).transpose(1, 0, 2).reshape(128, 16 * 33))
        maps.append(dict(
            x=np.ascontiguousarray(x[b]),
            xmy=np.ascontiguousarray(x[b, dh * MYT:(dh + 1) * MYT]),
            wip=wip,
            wxp=wxpc,
            wop=wop,
            convw=pcols(d["L0_attn_conv_w"][sl, 0, :]),
            convb=pcols(d["L0_attn_conv_b"][sl]),
            dtw=pcols(d["L0_attn_dt_w"][sl, 0]),
            dtb=pcols(d["L0_attn_dt_b"][sl]),
            alog=pcols(d["L0_attn_A_log"][sl]),
            dpar=pcols(d["L0_attn_D_param"][sl]),
            nw=pcols(d["L0_attn_norm_w"]),
            wg=wg, wu=wu, wd=wd,
            sg=pcols(d["L0_ffn_s_gate"]),
            su=pcols(d["L0_ffn_s_up"]),
            sd=pcols(d["L0_ffn_s_down"]),
            idbf=idbf,
        ))

    if "nc" not in _NC_CACHE:
        _NC_CACHE["nc"] = build("full")
    nc = _NC_CACHE["nc"]
    res = run_bass_kernel_spmd(nc, maps, core_ids=list(range(8)))
    out = np.empty((B, T, D), np.float32)
    for cid in range(8):
        b, dh = cid // 2, cid % 2
        out[b, dh * MYT:(dh + 1) * MYT] = res.results[cid]["o"]
    return out


# version 35
# speedup vs baseline: 1.1762x; 1.1762x over previous
import contextlib
import sys

import numpy as np
import ml_dtypes

sys.path.insert(0, "/opt/trn_rl_repo")
import concourse.bass as bass
import concourse.mybir as mybir
from concourse.bass_utils import run_bass_kernel_spmd

F32 = mybir.dt.float32
BF16 = mybir.dt.bfloat16
U8 = mybir.dt.uint8
AF = mybir.ActivationFunctionType
ALU = mybir.AluOpType
BF = ml_dtypes.bfloat16

B, T, D, DI, DIH, NST, HID = 4, 4096, 2048, 4096, 2048, 16, 5632
NTT = T // 128
NDT = DIH // 128
NKD = D // 128
NCH = 8
TC = T // NCH
MYT = T // 2
NHT = HID // 128
RG = [[0, 1], [2, 3], [4, 5], [6, 7]]


class Sem:
    def __init__(self, ctx, nc, name, step=1):
        self.h = ctx.enter_context(nc.semaphore(name))
        self.n = 0
        self.step = step

    def inc(self, instr, k=None):
        k = self.step if k is None else k
        instr.then_inc(self.h, k)
        self.n += k
        return instr



def wslice_dma(eng, dst_sb, src_ap, idx, width):
    return eng.dma_start(out=dst_sb[:],
                         in_=src_ap[:, idx * width:(idx + 1) * width])

def build(stop="full"):
    nc = bass.Bass("TRN2", target_bir_lowering=False, debug=False)

    ap_x = nc.dram_tensor("x", [T, D], F32, kind="ExternalInput").ap()
    ap_xmy = nc.dram_tensor("xmy", [MYT, D], F32, kind="ExternalInput").ap()
    ap_wip = nc.dram_tensor("wip", [128, 32 * 16 * 128], F32,
                            kind="ExternalInput").ap()
    ap_wxp = nc.dram_tensor("wxp", [128, 16 * 33], F32, kind="ExternalInput").ap()
    ap_wop = nc.dram_tensor("wop", [128, 16 * 32 * 128], F32,
                            kind="ExternalInput").ap()
    ap_convw = nc.dram_tensor("convw", [128, NDT * 4], F32, kind="ExternalInput").ap()
    ap_convb = nc.dram_tensor("convb", [128, NDT], F32, kind="ExternalInput").ap()
    ap_dtw = nc.dram_tensor("dtw", [128, NDT], F32, kind="ExternalInput").ap()
    ap_dtb = nc.dram_tensor("dtb", [128, NDT], F32, kind="ExternalInput").ap()
    ap_alog = nc.dram_tensor("alog", [128, NDT * NST], F32, kind="ExternalInput").ap()
    ap_dpar = nc.dram_tensor("dpar", [128, NDT], F32, kind="ExternalInput").ap()
    ap_nw = nc.dram_tensor("nw", [128, NKD], F32, kind="ExternalInput").ap()
    ap_wg = nc.dram_tensor("wg", [128, NHT * 16 * 128], U8,
                           kind="ExternalInput").ap()
    ap_wu = nc.dram_tensor("wu", [128, NHT * 16 * 128], U8,
                           kind="ExternalInput").ap()
    ap_wd = nc.dram_tensor("wd", [128, 16 * NHT * 128], U8,
                           kind="ExternalInput").ap()
    ap_sg = nc.dram_tensor("sg", [128, NHT], F32, kind="ExternalInput").ap()
    ap_su = nc.dram_tensor("su", [128, NHT], F32, kind="ExternalInput").ap()
    ap_sd = nc.dram_tensor("sd", [128, NKD], F32, kind="ExternalInput").ap()
    ap_idbf = nc.dram_tensor("idbf", [128, 128], BF16, kind="ExternalInput").ap()
    ap_o = nc.dram_tensor("o", [MYT, D], F32, kind="ExternalOutput").ap()

    xc_hbm = nc.dram_tensor("xc_hbm", [DIH, T], BF16).ap()
    z_hbm = nc.dram_tensor("z_hbm", [DIH, T], BF16).ap()
    x1_hbm = nc.dram_tensor("x1_hbm", [MYT, D], F32).ap()
    fin_hbm = nc.dram_tensor("fin_hbm", [D, MYT], BF16).ap()
    ar_in = [nc.dram_tensor(f"ar_in{i}", [33, MYT], F32).ap() for i in range(2)]
    ar_out = [nc.dram_tensor(f"ar_out{i}", [33, MYT], F32).ap()
              for i in range(2)]
    ar_bfB = nc.dram_tensor("ar_bfB", [NCH * NST, TC], BF16).ap()
    ar_bfC = nc.dram_tensor("ar_bfC", [NCH * NST, TC], BF16).ap()
    ag_in = [nc.dram_tensor(f"ag_in{c}", [128, NDT * TC], BF16).ap()
             for c in range(NCH)]
    ag_out = [nc.dram_tensor(f"ag_out{c}", [256, NDT * TC], BF16).ap()
              for c in range(NCH)]

    dbg = {}
    if stop in ("A", "S", "O"):
        dbg["dbg_xc"] = nc.dram_tensor("dbg_xc", [DIH, T], BF16, kind="ExternalOutput").ap()
        dbg["dbg_z"] = nc.dram_tensor("dbg_z", [DIH, T], BF16, kind="ExternalOutput").ap()
        dbg["dbg_xssm"] = nc.dram_tensor("dbg_xssm", [33, T], F32, kind="ExternalOutput").ap()
    if stop in ("S", "O"):
        dbg["dbg_yg"] = nc.dram_tensor("dbg_yg", [DIH, T], BF16, kind="ExternalOutput").ap()
    if stop == "O":
        dbg["dbg_x1"] = nc.dram_tensor("dbg_x1", [MYT, D], F32, kind="ExternalOutput").ap()
        dbg["dbg_fin"] = nc.dram_tensor("dbg_fin", [D, MYT], BF16, kind="ExternalOutput").ap()
    if stop == "F":
        dbg["dbg_gu"] = nc.dram_tensor("dbg_gu", [HID, MYT], BF16, kind="ExternalOutput").ap()
        dbg["dbg_fo"] = nc.dram_tensor("dbg_fo", [D, MYT], BF16, kind="ExternalOutput").ap()

    ctx = contextlib.ExitStack()
    with ctx:
        def sb(name, shape, dt, stack=None):
            return (stack or ctx).enter_context(nc.sbuf_tensor(name, shape, dt))

        nw_sb = sb("nw_sb", [128, NKD], F32)
        convw_sb = sb("convw_sb", [128, NDT * 4], F32)
        convb_sb = sb("convb_sb", [128, NDT], F32)
        dtw_sb = sb("dtw_sb", [128, NDT], F32)
        dtb_sb = sb("dtb_sb", [128, NDT], F32)
        dpar_sb = sb("dpar_sb", [128, NDT], F32)
        a_sb = sb("a_sb", [128, NDT * NST], F32)
        sg_sb = sb("sg_sb", [128, NHT], F32)
        su_sb = sb("su_sb", [128, NHT], F32)
        sd_sb = sb("sd_sb", [128, NKD], F32)
        idbf_sb = sb("idbf_sb", [128, 128], BF16)
        m_all = sb("m_all", [128, NTT], F32)
        r_all = sb("r_all", [128, NTT], F32)
        s1_all = sb("s1_all", [128, NTT], F32)
        s2_all = sb("s2_all", [128, NTT], F32)
        rq_all = sb("rq_all", [128, NTT], F32)
        vt1 = sb("vt1", [128, NTT], F32)
        vt2 = sb("vt2", [128, NTT], F32)
        ost = sb("ost", [128, 16], F32)
        xssm_part = sb("xssm_part", [33, T], F32)
        wxp_sb = sb("wxp_sb", [128, NKD * 33], BF16)
        carry = sb("carry", [128, NDT * NST], F32)

        pmm = [ctx.enter_context(nc.psum_tensor(f"pmm{i}", [128, 512], F32))
               for i in range(5)]
        paux = ctx.enter_context(nc.psum_tensor("paux", [128, 512], F32))
        ptb = [ctx.enter_context(nc.psum_tensor(f"ptb{i}", [128, 1024], BF16))
               for i in range(2)]

        ld = Sem(ctx, nc, "ld", step=16)
        cst = Sem(ctx, nc, "cst", step=16)
        yf = Sem(ctx, nc, "yf", step=16)
        rp = Sem(ctx, nc, "rp", step=16)
        xz = Sem(ctx, nc, "xz", step=16)
        gld = Sem(ctx, nc, "gld", step=16)
        pe = Sem(ctx, nc, "pe")
        dv = Sem(ctx, nc, "dv")
        ac = Sem(ctx, nc, "ac")
        gp = Sem(ctx, nc, "gp")
        po = Sem(ctx, nc, "po")
        cc = Sem(ctx, nc, "cc")
        agS = [Sem(ctx, nc, f"ag{c}") for c in range(NCH)]
        bar = Sem(ctx, nc, "bar")
        st = Sem(ctx, nc, "st", step=16)
        block = ctx.enter_context(nc.Block())

        E = {"sync": block.sync, "tensor": block.tensor, "vector": block.vector,
             "scalar": block.scalar, "gpsimd": block.gpsimd}

        def barrier():
            def _(sync, w=ld.n):
                sync.wait_ge(ld.h, w)
            E["sync"](_)
            def _(g, w=gld.n):
                g.wait_ge(gld.h, w)
            E["gpsimd"](_)
            target = bar.n + 5
            for e in E.values():
                def _(eng, t=target):
                    eng.nop().then_inc(bar.h, 1)
                    eng.wait_ge(bar.h, t)
                e(_)
            bar.n = target

        def _(sync):
            for dst, src in ((nw_sb, ap_nw), (convb_sb, ap_convb),
                             (dtw_sb, ap_dtw), (dtb_sb, ap_dtb),
                             (dpar_sb, ap_dpar), (sg_sb, ap_sg),
                             (su_sb, ap_su), (sd_sb, ap_sd),
                             (convw_sb, ap_convw), (a_sb, ap_alog)):
                cst.inc(sync.dma_start(out=dst[:], in_=src))
            cst.inc(sync.dma_start(out=idbf_sb[:], in_=ap_idbf))
        E["sync"](_)
        consts_cst = cst.n

        def _(g):
            gld.inc(wslice_dma(g, wxp_sb, ap_wxp, 0, 16 * 33))
            gp.inc(g.memset(carry[:], 0.0))
        E["gpsimd"](_)

        def _(s, w=cst.n):
            s.wait_ge(cst.h, w)
            ac.inc(s.activation(a_sb[:], a_sb[:], AF.Exp, bias=0.0, scale=1.0))
        E["scalar"](_)
        def _(v, w=ac.n, w2=gp.n):
            v.wait_ge(ac.h, w)
            v.wait_ge(gp.h, w2)
            dv.inc(v.tensor_scalar_mul(a_sb[:], a_sb[:], -1.0))
        E["vector"](_)
        consts_dv, consts_gld = dv.n, gld.n

        with contextlib.ExitStack() as actx:
            x_tm = [sb(f"x_tm{i}", [128, D], F32, actx) for i in range(2)]
            xscr = sb("xscr", [128, D], F32, actx)
            xn_tm = [sb(f"xn_tm{i}", [128, D], BF16, actx) for i in range(2)]
            xn_mega = sb("xn_mega", [128, NKD * 2048], BF16, actx)
            wbuf = [sb(f"wbuf{i}", [128, D], BF16, actx) for i in range(2)]
            xi_ext = sb("xi_ext", [128, 2048 + 3], BF16, actx)
            xi_tail = sb("xi_tail", [128, NDT * 3], BF16, actx)
            conv_acc = [sb(f"conv_acc{i}", [128, 512], F32, actx) for i in range(2)]
            xc_mega = sb("xc_mega", [128, NDT * 2048], BF16, actx)
            zbuf = [sb(f"zbuf{i}", [128, 512], BF16, actx) for i in range(2)]

            def _(v):
                dv.inc(v.memset(xi_tail[:], 0.0))
            E["vector"](_)

            xn_done = {}
            xn_half = [0, 0]
            paux_free = [None]
            evac_grp_dv = []
            mm_done_pe = {}
            bank_free = {}
            conv_dv = {}
            ca_silu_ac = [0, 0]
            z_dma_by_buf = [0, 0]
            xc_dma_by_slot = {}
            mega_mm_end = 0
            mega_xproj_pe = 0
            mega_xc_ld = 0
            ar_cc_half = [0, 0]
            ar_cast_half = [0, 0]

            for mega in range(2):
                for tt in range(16):
                    g_tt = mega * 16 + tt
                    xb = x_tm[g_tt % 2]
                    xnb = xn_tm[g_tt % 2]

                    def _(sync, xb=xb, g_tt=g_tt):
                        if g_tt >= 2:
                            sync.wait_ge(dv.h, xn_done[g_tt - 2])
                        ld.inc(sync.dma_start(
                            out=xb[:], in_=ap_x[g_tt * 128:(g_tt + 1) * 128, :]))
                    E["sync"](_)

                    def _(v, xb=xb, g_tt=g_tt, w=ld.n):
                        v.wait_ge(ld.h, w)
                        c = slice(g_tt, g_tt + 1)
                        v.tensor_scalar(xscr[:], xb[:], 1.0, 0.0, ALU.mult, ALU.add,
                                        accum_out=s1_all[:, c])
                        v.scalar_tensor_tensor(xscr[:], xb[:], 1.0, xb[:],
                                               ALU.mult, ALU.mult,
                                               accum_out=s2_all[:, c])
                        v.tensor_scalar_mul(m_all[:, c], s1_all[:, c], 1.0 / D)
                        v.tensor_scalar_mul(vt1[:, c], s2_all[:, c], 1.0 / D)
                        v.tensor_tensor(vt2[:, c], m_all[:, c], m_all[:, c], ALU.mult)
                        v.tensor_tensor(vt1[:, c], vt1[:, c], vt2[:, c], ALU.subtract)
                        v.tensor_scalar(vt2[:, c], vt1[:, c], 1.0 + 1e-5, 1e-10,
                                        ALU.mult, ALU.add)
                        dv.inc(v.tensor_scalar(vt1[:, c], vt1[:, c], 1.0, 1e-5,
                                               ALU.mult, ALU.add))
                    E["vector"](_)

                    def _(s, g_tt=g_tt, w=dv.n):
                        c = slice(g_tt, g_tt + 1)
                        s.wait_ge(dv.h, w)
                        s.activation(r_all[:, c], vt1[:, c], AF.Sqrt, bias=0.0,
                                     scale=1.0)
                        ac.inc(s.activation(rq_all[:, c], vt2[:, c], AF.Sqrt,
                                            bias=0.0, scale=1.0))
                    E["scalar"](_)

                    def _(v, xb=xb, xnb=xnb, g_tt=g_tt, w=ac.n):
                        c = slice(g_tt, g_tt + 1)
                        v.wait_ge(ac.h, w)
                        v.reciprocal(r_all[:, c], r_all[:, c])
                        v.reciprocal(rq_all[:, c], rq_all[:, c])
                        dv.inc(v.tensor_scalar(xnb[:], xb[:], m_all[:, c],
                                               rq_all[:, c], ALU.subtract, ALU.mult))
                    E["vector"](_)
                    xn_done[g_tt] = dv.n

                    for grp in range(2):
                        pb = ptb[grp]
                        def _(t, pb=pb, xnb=xnb, grp=grp, w=dv.n,
                              gi=len(evac_grp_dv), wc=consts_cst):
                            t.wait_ge(dv.h, w)
                            if gi >= 2:
                                t.wait_ge(dv.h, evac_grp_dv[gi - 2])
                            for blk in range(8):
                                db = grp * 8 + blk
                                ins = t.transpose(
                                    pb[:, blk * 128:(blk + 1) * 128],
                                    xnb[:, db * 128:(db + 1) * 128], idbf_sb[:])
                            pe.inc(ins)
                        E["tensor"](_)

                        def _(v, pb=pb, grp=grp, tt=tt, w=pe.n, wp=mega_mm_end):
                            v.wait_ge(pe.h, w)
                            if wp:
                                v.wait_ge(pe.h, wp)
                            src = pb[:].rearrange("p (b c) -> p b c", b=8)
                            dst = xn_mega[:].rearrange("p (dt c) -> p dt c", dt=NKD)[
                                :, grp * 8:(grp + 1) * 8, tt * 128:(tt + 1) * 128]
                            dv.inc(v.tensor_copy(dst, src))
                        E["vector"](_)
                        evac_grp_dv.append(dv.n)
                    if tt in (7, 15):
                        hh = 0 if tt == 7 else 1
                        def _(v, hh=hh, w2=consts_dv):
                            if hh == 0:
                                v.wait_ge(dv.h, w2)
                            xnm = xn_mega[:].rearrange("p (dt c) -> p dt c",
                                                       dt=NKD)
                            for db in range(NKD):
                                ins = v.tensor_scalar_mul(
                                    xnm[:, db, hh * 1024:(hh + 1) * 1024],
                                    xnm[:, db, hh * 1024:(hh + 1) * 1024],
                                    nw_sb[:, db:db + 1])
                            dv.inc(ins)
                        E["vector"](_)
                        xn_half[hh] = dv.n


                for mt in range(32):
                    gmt = mega * 32 + mt
                    wb = wbuf[mt % 2]
                    def _(g, wb=wb, mt=mt, gmt=gmt):
                        if gmt >= 2:
                            g.wait_ge(pe.h, mm_done_pe[gmt - 2])
                        gld.inc(wslice_dma(g, wb, ap_wip, mt, 2048))
                    E["gpsimd"](_)
                    w_ld = gld.n

                    tail_dv = None
                    if mt < 16:
                        def _(v, mt=mt):
                            if mt >= 1:
                                v.wait_ge(dv.h, conv_dv[mt - 1])
                            dv.inc(v.tensor_copy(
                                xi_ext[:, 0:3], xi_tail[:, mt * 3:(mt + 1) * 3]))
                        E["vector"](_)
                        tail_dv = dv.n

                    bids = [(mt * 4 + i) % 5 for i in range(4)]
                    banks4 = [pmm[i] for i in bids]
                    mm_pe_half = [0, 0]
                    for hh in range(2):
                        def _(t, wb=wb, hh=hh, w=w_ld, w2=xn_half[hh],
                              banks4=banks4,
                              wb_=[bank_free.get(i) for i in bids[hh * 2:
                                                                 hh * 2 + 2]]):
                            t.wait_ge(gld.h, w)
                            t.wait_ge(dv.h, w2)
                            for i in range(2):
                                if wb_[i]:
                                    t.wait_ge(wb_[i][0].h, wb_[i][1])
                            for kt in range(NKD):
                                for nch in (hh * 2, hh * 2 + 1):
                                    ins = t.matmul(
                                        banks4[nch][:, 0:512],
                                        wb[:, kt * 128:(kt + 1) * 128],
                                        xn_mega[:, kt * 2048 + nch * 512:
                                                kt * 2048 + nch * 512 + 512],
                                        start=(kt == 0), stop=(kt == NKD - 1))
                            pe.inc(ins)
                        E["tensor"](_)
                        mm_pe_half[hh] = pe.n

                    for nch in range(4):
                        bank_i = bids[nch]
                        bank = banks4[nch]
                        mm_pe = mm_pe_half[nch // 2]

                        if mt < 16:
                            ca = conv_acc[nch % 2]
                            def _(v, bank=bank, ca=ca, mt=mt, nch=nch, w=mm_pe,
                                  wc=ca_silu_ac[nch % 2]):
                                v.wait_ge(pe.h, w)
                                v.tensor_copy(
                                    xi_ext[:, 3 + nch * 512:3 + nch * 512 + 512],
                                    bank[:, 0:512])
                                if wc:
                                    v.wait_ge(ac.h, wc)
                                dv.inc(v.tensor_scalar_mul(
                                    ca[:], xi_ext[:, nch * 512:nch * 512 + 512],
                                    convw_sb[:, mt * 4:mt * 4 + 1]))
                            E["vector"](_)
                            bank_free[bank_i] = (dv, dv.n)
                            def _(v, ca=ca, mt=mt, nch=nch):
                                for k in range(1, 4):
                                    ins = v.scalar_tensor_tensor(
                                        ca[:],
                                        xi_ext[:, nch * 512 + k:nch * 512 + k + 512],
                                        convw_sb[:, mt * 4 + k:mt * 4 + k + 1],
                                        ca[:], ALU.mult, ALU.add)
                                dv.inc(ins)
                            E["vector"](_)
                            def _(s, ca=ca, mt=mt, nch=nch, w=dv.n, mega=mega,
                                  wl=xc_dma_by_slot.get((mt, nch), 0),
                                  wp=mega_xproj_pe):
                                s.wait_ge(dv.h, w)
                                if mega == 1 and wl:
                                    s.wait_ge(ld.h, wl)
                                if mega == 1 and wp:
                                    s.wait_ge(pe.h, wp)
                                ac.inc(s.activation(
                                    xc_mega[:, mt * 2048 + nch * 512:
                                            mt * 2048 + nch * 512 + 512],
                                    ca[:], AF.Silu,
                                    bias=convb_sb[:, mt:mt + 1], scale=1.0))
                            E["scalar"](_)
                            ca_silu_ac[nch % 2] = ac.n
                            def _(sync, mt=mt, nch=nch, mega=mega, w=ac.n):
                                sync.wait_ge(ac.h, w)
                                col = mega * 2048 + nch * 512
                                ld.inc(sync.dma_start(
                                    out=xc_hbm[mt * 128:(mt + 1) * 128, col:col + 512],
                                    in_=xc_mega[:, mt * 2048 + nch * 512:
                                                mt * 2048 + nch * 512 + 512]))
                            E["sync"](_)
                            xc_dma_by_slot[(mt, nch)] = ld.n
                        else:
                            zb = zbuf[nch % 2]
                            def _(v, bank=bank, zb=zb, w=mm_pe,
                                  wl=z_dma_by_buf[nch % 2]):
                                v.wait_ge(pe.h, w)
                                if wl:
                                    v.wait_ge(ld.h, wl)
                                dv.inc(v.tensor_copy(zb[:], bank[:, 0:512]))
                            E["vector"](_)
                            bank_free[bank_i] = (dv, dv.n)
                            def _(sync, zb=zb, mt=mt, nch=nch, mega=mega, w=dv.n):
                                sync.wait_ge(dv.h, w)
                                col = mega * 2048 + nch * 512
                                ld.inc(sync.dma_start(
                                    out=z_hbm[(mt - 16) * 128:(mt - 15) * 128,
                                              col:col + 512],
                                    in_=zb[:]))
                            E["sync"](_)
                            z_dma_by_buf[nch % 2] = ld.n

                    mm_done_pe[gmt] = pe.n

                    if mt < 16:
                        def _(v, mt=mt):
                            dv.inc(v.tensor_copy(
                                xi_tail[:, mt * 3:(mt + 1) * 3],
                                xi_ext[:, 2048:2051]))
                        E["vector"](_)
                        conv_dv[mt] = dv.n

                    if mt == 15:
                        for nch in range(4):
                            def _(t, nch=nch, w=ac.n, wg=consts_gld,
                                  wb_=paux_free[0]):
                                t.wait_ge(ac.h, w)
                                t.wait_ge(gld.h, wg)
                                if wb_:
                                    t.wait_ge(wb_[0].h, wb_[1])
                                for kt in range(NDT):
                                    ins = t.matmul(
                                        paux[0:33, 0:512],
                                        wxp_sb[:, kt * 33:(kt + 1) * 33],
                                        xc_mega[:, kt * 2048 + nch * 512:
                                                kt * 2048 + nch * 512 + 512],
                                        start=(kt == 0), stop=(kt == NDT - 1))
                                pe.inc(ins)
                            E["tensor"](_)
                            def _(v, nch=nch, mega=mega, w=pe.n):
                                v.wait_ge(pe.h, w)
                                col = mega * 2048 + nch * 512
                                dv.inc(v.tensor_copy(
                                    xssm_part[:, col:col + 512],
                                    paux[0:33, 0:512]))
                            E["vector"](_)
                            paux_free[0] = (dv, dv.n)
                        mega_xproj_pe = pe.n

                        def _(g, mega=mega, w=dv.n):
                            g.wait_ge(dv.h, w)
                            gld.inc(g.dma_start(
                                out=ar_in[mega],
                                in_=xssm_part[:, mega * MYT:(mega + 1) * MYT]))
                        E["gpsimd"](_)
                        def _(g, mega=mega, w=gld.n):
                            g.wait_ge(gld.h, w)
                            cc.inc(g.collective_compute(
                                "AllReduce", ALU.add, replica_groups=RG,
                                ins=[ar_in[mega].opt()],
                                outs=[ar_out[mega].opt()]))
                        E["gpsimd"](_)
                        ar_cc_half[mega] = cc.n
                        def _(g, mega=mega, w=cc.n):
                            g.wait_ge(cc.h, w)
                            gld.inc(g.dma_start(
                                out=ar_bfB[mega * 4 * NST:(mega + 1) * 4 * NST]
                                .rearrange("(ch n) c -> n ch c", n=NST),
                                in_=ar_out[mega][0:NST, :]
                                .rearrange("n (ch c) -> n ch c", ch=4)))
                            gld.inc(g.dma_start(
                                out=ar_bfC[mega * 4 * NST:(mega + 1) * 4 * NST]
                                .rearrange("(ch n) c -> n ch c", n=NST),
                                in_=ar_out[mega][NST:2 * NST, :]
                                .rearrange("n (ch c) -> n ch c", ch=4)))
                        E["gpsimd"](_)
                        ar_cast_half[mega] = gld.n

                mega_mm_end = pe.n

            if stop == "A":
                def _(g, w=ld.n):
                    g.wait_ge(ld.h, w)
                    gld.inc(g.dma_start(out=dbg["dbg_xc"], in_=xc_hbm))
                    gld.inc(g.dma_start(out=dbg["dbg_z"], in_=z_hbm))
                E["gpsimd"](_)
                def _(sync, w=dv.n):
                    sync.wait_ge(dv.h, w)
                    ld.inc(sync.dma_start(out=dbg["dbg_xssm"], in_=xssm_part[:]))
                E["sync"](_)
                def _(g, w1=gld.n, w2=ld.n):
                    g.wait_ge(gld.h, w1)
                    g.wait_ge(ld.h, w2)
                E["gpsimd"](_)
                return nc

            barrier()

        with contextlib.ExitStack() as sctx:
            SCW = NST * 513
            xcs = sb("xcs", [128, NDT * TC], BF16, sctx)
            zsb = sb("zsb", [128, NDT * TC], BF16, sctx)
            srep2 = [sb(f"srep{i}", [128, TC], F32, sctx) for i in range(2)]
            brep2 = [sb(f"brep{i}", [128, NST * TC], BF16, sctx)
                     for i in range(2)]
            crep1 = sb("crep", [128, NST * TC], BF16, sctx)
            crep2 = [crep1, crep1]
            w2b = [sb(f"w2b{i}", [128, TC], F32, sctx) for i in range(2)]
            w3b = [sb(f"w3b{i}", [128, TC], F32, sctx) for i in range(2)]
            dt3 = [sb(f"dt3_{i}", [128, TC], F32, sctx) for i in range(3)]
            xt3 = [sb(f"xt3_{i}", [128, TC], BF16, sctx) for i in range(3)]
            ydu = sb("ydu", [128, TC], BF16, sctx)
            da_all = [sb(f"da_all{i}", [128, SCW], BF16, sctx) for i in range(2)]
            xb_all = [sb(f"xb_all{i}", [128, SCW], BF16, sctx) for i in range(2)]
            yacc = sb("yacc", [128, TC], F32, sctx)
            fs2 = sb("fs2", [128, 4 * TC], BF16, sctx)
            fs3 = sb("fs3", [128, 2 * TC], BF16, sctx)

            yg_t = [sb(f"yg_t{i}", [128, TC], BF16, sctx) for i in range(2)]

            def _(v):
                v.memset(da_all[0][:], 0.0)
                dv.inc(v.memset(da_all[1][:], 0.0))
            E["vector"](_)

            rep_free_dv = [0, 0]
            load_used_po = {}
            ygbuf_free_ld = [0, 0]
            zs_free_po2 = 0
            da_free = [0, 0]
            fs_free_po = [0]
            tail_po = {}
            qld_of = {}
            silu_q = {}
            dtxt_free = [0, 0]
            sc_run = {}
            vdt_dv = {}
            v3_of = {}

            def sc_da_wait(ch, dtile, da_free, v3_of):
                return da_free[(dtile - 1) % 2] if dtile >= 1 else 0

            def emit_v3(ch, t):
                pp = t % 2
                dab, xbb = da_all[pp], xb_all[pp]
                def _(v, ch=ch, t=t, pp=pp, dab=dab, xbb=xbb,
                      w=sc_run[(ch, t + 1)],
                      wf=fs_free_po[0]):
                    v.wait_ge(ac.h, w)
                    if wf:
                        v.wait_ge(po.h, wf)
                    xbv = xbb[:].rearrange("p (n c) -> p n c", c=513)
                    hv = xbv
                    v.tensor_tensor(
                        xbv[:, :, 1:513],
                        xt3[t % 3][:].unsqueeze(1).broadcast_to([128, NST, TC]),
                        brep[:].rearrange("p (n c) -> p n c", n=NST),
                        ALU.mult)
                    v.tensor_tensor_scan(xbb[:], dab[:], xbb[:],
                                         0.0, ALU.mult, ALU.add)
                    v.tensor_copy(
                        carry[:, t * NST:(t + 1) * NST]
                        .rearrange("p (n one) -> p n one", one=1),
                        hv[:, :, 512:513])
                    v.tensor_tensor(
                        dab[:, 0:NST * TC].rearrange("p (n c) -> p n c",
                                                     n=NST),
                        hv[:, :, 1:513],
                        crep[:].rearrange("p (n c) -> p n c", n=NST),
                        ALU.mult)
                    v.tensor_tensor(
                        xbb[:, 0:8 * TC], dab[:, 0:8 * TC],
                        dab[:, 8 * TC:16 * TC], ALU.add)
                    dv.inc(v.tensor_tensor(
                        fs2[:], xbb[:, 0:4 * TC],
                        xbb[:, 4 * TC:8 * TC], ALU.add))
                E["vector"](_)
                v3_of[(ch, t)] = dv.n
                da_free[pp] = dv.n
                dtxt_free[pp] = dv.n

                ygb = yg_t[pp]
                def _(g, ch=ch, t=t, ygb=ygb, w=dv.n,
                      wz=silu_q[(ch, t // 4)],
                      wl=ygbuf_free_ld[pp]):
                    g.wait_ge(dv.h, w)
                    g.tensor_tensor(fs3[:], fs2[:, 0:2 * TC],
                                    fs2[:, 2 * TC:4 * TC], ALU.add)
                    g.tensor_tensor(yacc[:], fs3[:, 0:TC],
                                    fs3[:, TC:2 * TC], ALU.add)
                    g.tensor_tensor(
                        ydu[:], xcs[:, t * TC:(t + 1) * TC],
                        dpar_sb[:, t:t + 1].broadcast_to([128, TC]),
                        ALU.mult)
                    g.tensor_tensor(yacc[:], yacc[:], ydu[:], ALU.add)
                    if wl:
                        g.wait_ge(ld.h, wl)
                    po.inc(g.tensor_tensor(
                        ygb[:], yacc[:],
                        zsb[:, t * TC:(t + 1) * TC], ALU.mult))
                E["gpsimd"](_)
                fs_free_po[0] = po.n
                tail_po[(ch, t)] = po.n

                def _(sync, ch=ch, t=t, ygb=ygb, w=po.n):
                    sync.wait_ge(po.h, w)
                    ld.inc(sync.dma_start(
                        out=ag_in[ch][:, t * TC:(t + 1) * TC],
                        in_=ygb[:]))
                    if stop in ("S", "O"):
                        ld.inc(sync.dma_start(
                            out=dbg["dbg_yg"][t * 128:(t + 1) * 128,
                                              ch * TC:(ch + 1) * TC],
                            in_=ygb[:]))
                E["sync"](_)
                ygbuf_free_ld[pp] = ld.n

            for ch in range(NCH):
                cols = slice(ch * TC, (ch + 1) * TC)
                xcb = xcs

                half = ch // 4
                cpp = ch % 2
                brep, crep, srep = brep2[cpp], crep2[cpp], srep2[cpp]
                hcols = slice((ch % 4) * TC, (ch % 4 + 1) * TC)
                def _(s, ch=ch, half=half, hcols=hcols, cpp=cpp,
                      w=ar_cast_half[ch // 4], w2=rep_free_dv[cpp],
                      w3=rep_free_dv[1 - cpp]):
                    if ch % 4 == 0:
                        s.wait_ge(gld.h, w)
                    if w2:
                        s.wait_ge(dv.h, w2)
                    rp.inc(s.dma_start(
                        out=brep2[cpp][:],
                        in_=ar_bfB[ch * NST:(ch + 1) * NST, :]
                        .rearrange("n c -> () (n c)")
                        .broadcast_to([128, NST * TC])))
                    rp.inc(s.dma_start(
                        out=srep2[cpp][:],
                        in_=ar_out[half][32:33, hcols].broadcast_to([128, TC])))
                    if w3:
                        s.wait_ge(dv.h, w3)
                    rp.inc(s.dma_start(
                        out=crep2[cpp][:],
                        in_=ar_bfC[ch * NST:(ch + 1) * NST, :]
                        .rearrange("n c -> () (n c)")
                        .broadcast_to([128, NST * TC])))
                E["scalar"](_)
                rep_gld = rp.n

                for q in range(4):
                    def _(sync, ch=ch, cols=cols, q=q,
                          w=tail_po.get((ch - 1, q * 4 + 3), 0)):
                        if w:
                            sync.wait_ge(po.h, w)
                        xz.inc(sync.dma_start(
                            out=xcs[:, q * 4 * TC:(q + 1) * 4 * TC]
                            .rearrange("p (dt c) -> p dt c", dt=4),
                            in_=xc_hbm[:, cols]
                            .rearrange("(dt p) c -> p dt c", p=128)
                            [:, q * 4:(q + 1) * 4, :]))
                        xz.inc(sync.dma_start(
                            out=zsb[:, q * 4 * TC:(q + 1) * 4 * TC]
                            .rearrange("p (dt c) -> p dt c", dt=4),
                            in_=z_hbm[:, cols]
                            .rearrange("(dt p) c -> p dt c", p=128)
                            [:, q * 4:(q + 1) * 4, :]))
                    E["sync"](_)
                    qld_of[(ch, q)] = xz.n
                chunk_ld = xz.n

                for q in range(4):
                    def _(s, ch=ch, q=q, w=qld_of[(ch, q)]):
                        s.wait_ge(xz.h, w)
                        ac.inc(s.activation(
                            zsb[:, q * 4 * TC:(q + 1) * 4 * TC],
                            zsb[:, q * 4 * TC:(q + 1) * 4 * TC],
                            AF.Silu, bias=0.0, scale=1.0))
                    E["scalar"](_)
                    silu_q[(ch, q)] = ac.n
                siluz_ac = ac.n

                def emit_softplus(s, d):
                    qq = d % 2
                    s.activation(w2b[qq][:], srep[:], AF.Abs,
                                 bias=dtb_sb[:, d:d + 1],
                                 scale=dtw_sb[:, d:d + 1])
                    s.activation(w2b[qq][:], w2b[qq][:], AF.Exp, bias=0.0,
                                 scale=-1.0)
                    s.activation(w2b[qq][:], w2b[qq][:], AF.Ln, bias=1.0,
                                 scale=1.0)
                    return s.activation(w3b[qq][:], srep[:], AF.Relu,
                                        bias=dtb_sb[:, d:d + 1],
                                        scale=dtw_sb[:, d:d + 1])

                def emit_vdt(d):
                    def _(v, d=d, p3=d % 3, qq=d % 2, w=ac.n,
                          wl=qld_of[(ch, d // 4)] if d % 4 <= 1 else 0):
                        v.wait_ge(ac.h, w)
                        if wl:
                            v.wait_ge(xz.h, wl)
                        v.tensor_tensor(dt3[p3][:], w3b[qq][:], w2b[qq][:],
                                        ALU.add)
                        dv.inc(v.tensor_tensor(
                            xt3[p3][:], dt3[p3][:],
                            xcb[:, d * TC:(d + 1) * TC], ALU.mult))
                    E["vector"](_)
                    vdt_dv[(ch, d)] = dv.n

                def _(s, ch=ch, w1=rep_gld):
                    s.wait_ge(rp.h, w1)
                    emit_softplus(s, 0)
                    ac.inc(emit_softplus(s, 1))
                E["scalar"](_)
                emit_vdt(0)

                for dtile in range(NDT):
                    def _(s, ch=ch, dtile=dtile,
                          wda=sc_da_wait(ch, dtile, da_free, v3_of),
                          wdt=vdt_dv.get((ch, dtile - 1), 0)):
                        if wda:
                            s.wait_ge(dv.h, wda)
                        if wdt:
                            s.wait_ge(dv.h, wdt)
                        ins = None
                        if dtile >= 1:
                            pd, ppp = dtile - 1, (dtile - 1) % 2
                            s.activation(
                                xb_all[ppp][:].rearrange(
                                    "p (n c) -> p n c", c=513)[:, :, 0:1],
                                carry[:, pd * NST:(pd + 1) * NST]
                                .rearrange("p (n one) -> p n one", one=1),
                                AF.Copy, bias=0.0, scale=1.0)
                            for n in range(NST):
                                ins = s.activation(
                                    da_all[ppp][:, n * 513 + 1:n * 513 + 513],
                                    dt3[pd % 3][:], AF.Exp, bias=0.0,
                                    scale=a_sb[:, pd * NST + n:pd * NST + n + 1])
                        if dtile + 2 < NDT:
                            ins = emit_softplus(s, dtile + 2)
                        ac.inc(ins)
                    E["scalar"](_)
                    sc_run[(ch, dtile)] = ac.n

                    if dtile + 1 < NDT:
                        emit_vdt(dtile + 1)
                    if dtile >= 1:
                        emit_v3(ch, dtile - 1)

                def _(s, ch=ch, wda=sc_da_wait(ch, NDT, da_free, v3_of),
                      wdt=vdt_dv[(ch, NDT - 1)]):
                    if wda:
                        s.wait_ge(dv.h, wda)
                    s.wait_ge(dv.h, wdt)
                    ppp = (NDT - 1) % 2
                    pd = NDT - 1
                    s.activation(
                        xb_all[ppp][:].rearrange(
                            "p (n c) -> p n c", c=513)[:, :, 0:1],
                        carry[:, pd * NST:(pd + 1) * NST]
                        .rearrange("p (n one) -> p n one", one=1),
                        AF.Copy, bias=0.0, scale=1.0)
                    for n in range(NST):
                        ins = s.activation(
                            da_all[ppp][:, n * 513 + 1:n * 513 + 513],
                            dt3[pd % 3][:], AF.Exp, bias=0.0,
                            scale=a_sb[:, (NDT - 1) * NST + n:
                                       (NDT - 1) * NST + n + 1])
                    ac.inc(ins)
                E["scalar"](_)
                sc_run[(ch, NDT)] = ac.n
                emit_v3(ch, NDT - 1)

                rep_free_dv[cpp] = v3_of[(ch, NDT - 1)]
                load_used_po[ch] = fs_free_po[0]
                zs_free_po2 = fs_free_po[0]

                def _(g, ch=ch, w=ld.n):
                    g.wait_ge(ld.h, w)
                    agS[ch].inc(g.collective_compute(
                        "AllGather", ALU.bypass, replica_groups=RG,
                        ins=[ag_in[ch].opt()], outs=[ag_out[ch].opt()]))
                E["gpsimd"](_)

            if stop == "S":
                def _(sync, w=ld.n):
                    sync.wait_ge(ld.h, w)
                E["sync"](_)
                def _(g, w=gld.n):
                    g.wait_ge(gld.h, w)
                    for c in range(NCH):
                        g.wait_ge(agS[c].h, agS[c].n)
                E["gpsimd"](_)
                return nc

            barrier()

        ptb_evac_global = [0, 0]
        with contextlib.ExitStack() as octx:
            yful = [sb(f"yful{i}", [128, 32 * TC], BF16, octx)
                    for i in range(2)]
            wopb = [sb(f"wopb{i}", [128, 32 * 128], BF16, octx) for i in range(2)]
            om_bf = sb("om_bf", [128, 16 * TC], BF16, octx)
            ot_tm = [sb(f"ot_tm{i}", [128, D], F32, octx) for i in range(2)]
            xrb = [sb(f"xrb{i}", [128, D], F32, octx) for i in range(2)]
            axb = sb("axb", [128, D], F32, octx)
            aob = sb("aob", [128, D], F32, octx)
            xscr2 = axb
            x1b = [sb(f"x1b{i}", [128, D], F32, octx) for i in range(2)]
            fnb = [sb(f"fnb{i}", [128, D], BF16, octx) for i in range(2)]
            fstage = sb("fstage", [128, NKD * 256], BF16, octx)

            wop_dma_by_buf = [0, 0]
            yful_used_pe = [0, 0, 0, 0]
            ptb_evac = ptb_evac_global
            x_used_dv = [0, 0]
            x1_used_ld = [0, 0]
            fn_used_ac = [0, 0]
            ot_used_dv = [0, 0]
            fst_dma_ld = [0]

            def load_yful(k):
                yb = yful[k % 2]
                def _(sync, k=k, yb=yb, w=yful_used_pe[k - 2] if k >= 2 else 0):
                    if w:
                        sync.wait_ge(pe.h, w)
                    for arm in nc.switch_core_id(engines=[sync], n=128):
                        dh = arm.logical % 2
                        g = dh * 4 + k
                        sync.wait_ge(agS[g].h, 1)
                        sync.dma_start(
                            out=yb[:].rearrange("p (r f) -> p r f", r=2),
                            in_=ag_out[g].rearrange("(r p) f -> p r f", p=128)
                        ).then_inc(yf.h, 16)
                E["sync"](_)
                yf.n += 16
                return yf.n

            ombf_dv_of = [0] * 4

            def emit_chains(kq):
                    for tt4 in range(4):
                        myrow = kq * 512 + tt4 * 128
                        otb = ot_tm[tt4 % 2]
                        xr = xrb[tt4 % 2]
                        x1t = x1b[tt4 % 2]
                        fnt = fnb[tt4 % 2]

                        def _(sync, xr=xr, myrow=myrow, w=x_used_dv[tt4 % 2]):
                            if w:
                                sync.wait_ge(dv.h, w)
                            ld.inc(sync.dma_start(
                                out=xr[:], in_=ap_xmy[myrow:myrow + 128, :]))
                        E["sync"](_)
                        xr_ld = ld.n

                        for grp in range(2):
                            pb = ptb[grp]
                            def _(t, pb=pb, grp=grp, tt4=tt4, w=ombf_dv_of[kq],
                                  w2=ot_used_dv[tt4 % 2], w3=ptb_evac[grp]):
                                t.wait_ge(dv.h, w)
                                if w2:
                                    t.wait_ge(dv.h, w2)
                                if w3:
                                    t.wait_ge(dv.h, w3)
                                for blk in range(8):
                                    mt = grp * 8 + blk
                                    ins = t.transpose(
                                        pb[:, blk * 128:(blk + 1) * 128],
                                        om_bf[:, mt * TC + tt4 * 128:
                                              mt * TC + tt4 * 128 + 128],
                                        idbf_sb[:])
                                pe.inc(ins)
                            E["tensor"](_)
                            def _(v, pb=pb, grp=grp, otb=otb, w=pe.n):
                                v.wait_ge(pe.h, w)
                                src = pb[:].rearrange("p (b c) -> p b c", b=8)
                                dst = otb[:].rearrange("p (mt c) -> p mt c", mt=16)[
                                    :, grp * 8:(grp + 1) * 8, :]
                                dv.inc(v.tensor_copy(dst, src))
                            E["vector"](_)
                            ptb_evac[grp] = dv.n

                        def _(v, xr=xr, w=xr_ld):
                            v.wait_ge(ld.h, w)
                            v.tensor_scalar(xscr2[:], xr[:], 1.0, 0.0, ALU.mult, ALU.add,
                                            accum_out=ost[:, 0:1])
                            v.scalar_tensor_tensor(xscr2[:], xr[:], 1.0, xr[:],
                                                   ALU.mult, ALU.mult,
                                                   accum_out=ost[:, 1:2])
                            v.tensor_scalar_mul(ost[:, 0:1], ost[:, 0:1], 1.0 / D)
                            v.tensor_scalar_mul(ost[:, 1:2], ost[:, 1:2], 1.0 / D)
                            v.tensor_tensor(ost[:, 2:3], ost[:, 0:1], ost[:, 0:1],
                                            ALU.mult)
                            v.tensor_tensor(ost[:, 1:2], ost[:, 1:2], ost[:, 2:3],
                                            ALU.subtract)
                            dv.inc(v.tensor_scalar(ost[:, 1:2], ost[:, 1:2], 1.0,
                                                   1e-5, ALU.mult, ALU.add))
                        E["vector"](_)
                        def _(s, w=dv.n):
                            s.wait_ge(dv.h, w)
                            ac.inc(s.activation(ost[:, 1:2], ost[:, 1:2], AF.Sqrt,
                                                bias=0.0, scale=1.0))
                        E["scalar"](_)
                        def _(v, w=ac.n):
                            v.wait_ge(ac.h, w)
                            dv.inc(v.reciprocal(ost[:, 1:2], ost[:, 1:2]))
                        E["vector"](_)
                        mr_ac = ac.n
                        mr_dv = dv.n

                        def _(v, xr=xr, otb=otb, x1t=x1t, w=mr_ac, w2=ac.n):
                            v.wait_ge(ac.h, w2)
                            v.tensor_scalar(axb[:], xr[:], ost[:, 0:1], ost[:, 1:2],
                                            ALU.subtract, ALU.mult)
                            v.tensor_tensor(aob[:], otb[:], axb[:], ALU.add)
                            v.tensor_scalar(xscr2[:], aob[:], 1.0, 0.0, ALU.mult, ALU.add,
                                            accum_out=ost[:, 4:5])
                            v.scalar_tensor_tensor(xscr2[:], aob[:], 1.0, aob[:],
                                                   ALU.mult, ALU.mult,
                                                   accum_out=ost[:, 5:6])
                            v.tensor_scalar_mul(ost[:, 4:5], ost[:, 4:5], 1.0 / D)
                            v.tensor_scalar_mul(ost[:, 5:6], ost[:, 5:6], 1.0 / D)
                            v.tensor_tensor(ost[:, 6:7], ost[:, 4:5], ost[:, 4:5],
                                            ALU.mult)
                            v.tensor_tensor(ost[:, 5:6], ost[:, 5:6], ost[:, 6:7],
                                            ALU.subtract)
                            dv.inc(v.tensor_scalar(ost[:, 5:6], ost[:, 5:6], 1.0,
                                                   1e-5, ALU.mult, ALU.add))
                        E["vector"](_)
                        def _(s, w=dv.n):
                            s.wait_ge(dv.h, w)
                            ac.inc(s.activation(ost[:, 5:6], ost[:, 5:6], AF.Sqrt,
                                                bias=0.0, scale=1.0))
                        E["scalar"](_)
                        def _(v, w=ac.n):
                            v.wait_ge(ac.h, w)
                            dv.inc(v.reciprocal(ost[:, 5:6], ost[:, 5:6]))
                        E["vector"](_)

                        def _(v, xr=xr, x1t=x1t, w=ac.n, wl=x1_used_ld[tt4 % 2]):
                            v.wait_ge(ac.h, w)
                            if wl:
                                v.wait_ge(ld.h, wl)
                            v.tensor_scalar(xscr2[:], aob[:], ost[:, 4:5],
                                            ost[:, 5:6], ALU.subtract, ALU.mult)
                            v.tensor_tensor(x1t[:], xscr2[:], xr[:], ALU.add)
                            v.tensor_scalar(xscr2[:], x1t[:], 1.0, 0.0, ALU.mult, ALU.add,
                                            accum_out=ost[:, 8:9])
                            v.scalar_tensor_tensor(xscr2[:], x1t[:], 1.0, x1t[:],
                                                   ALU.mult, ALU.mult,
                                                   accum_out=ost[:, 9:10])
                            v.tensor_scalar_mul(ost[:, 8:9], ost[:, 8:9], 1.0 / D)
                            v.tensor_scalar_mul(ost[:, 9:10], ost[:, 9:10], 1.0 / D)
                            v.tensor_tensor(ost[:, 10:11], ost[:, 8:9], ost[:, 8:9],
                                            ALU.mult)
                            v.tensor_tensor(ost[:, 9:10], ost[:, 9:10],
                                            ost[:, 10:11], ALU.subtract)
                            dv.inc(v.tensor_scalar(ost[:, 9:10], ost[:, 9:10], 1.0,
                                                   1e-5, ALU.mult, ALU.add))
                        E["vector"](_)
                        x_used_dv[tt4 % 2] = dv.n
                        def _(s, w=dv.n):
                            s.wait_ge(dv.h, w)
                            ac.inc(s.activation(ost[:, 9:10], ost[:, 9:10], AF.Sqrt,
                                                bias=0.0, scale=1.0))
                        E["scalar"](_)
                        def _(v, x1t=x1t, fnt=fnt, w=ac.n, wf=fn_used_ac[tt4 % 2]):
                            v.wait_ge(ac.h, w)
                            v.reciprocal(ost[:, 9:10], ost[:, 9:10])
                            dv.inc(v.tensor_scalar(fnt[:], x1t[:], ost[:, 8:9],
                                                   ost[:, 9:10], ALU.subtract,
                                                   ALU.mult))
                        E["vector"](_)
                        fn_dv = dv.n

                        def _(sync, x1t=x1t, myrow=myrow, w=fn_dv):
                            sync.wait_ge(dv.h, w)
                            ld.inc(sync.dma_start(
                                out=x1_hbm[myrow:myrow + 128, :], in_=x1t[:]))
                        E["sync"](_)
                        x1_used_ld[tt4 % 2] = ld.n

                        for grp in range(2):
                            pb = ptb[grp]
                            def _(t, pb=pb, fnt=fnt, grp=grp, w=fn_dv,
                                  wf=fst_dma_ld[0], w3=ptb_evac[grp]):
                                t.wait_ge(dv.h, w)
                                if wf:
                                    t.wait_ge(ld.h, wf)
                                if w3:
                                    t.wait_ge(dv.h, w3)
                                for blk in range(8):
                                    db = grp * 8 + blk
                                    ins = t.transpose(
                                        pb[:, blk * 128:(blk + 1) * 128],
                                        fnt[:, db * 128:(db + 1) * 128], idbf_sb[:])
                                pe.inc(ins)
                            E["tensor"](_)
                            def _(v, pb=pb, grp=grp, tt4=tt4, w=pe.n):
                                v.wait_ge(pe.h, w)
                                src = pb[:].rearrange("p (b c) -> p b c", b=8)
                                dst = fstage[:].rearrange(
                                    "p (db c) -> p db c", db=NKD)[
                                    :, grp * 8:(grp + 1) * 8,
                                    (tt4 % 2) * 128:(tt4 % 2) * 128 + 128]
                                dv.inc(v.tensor_copy(dst, src))
                            E["vector"](_)
                            ptb_evac[grp] = dv.n
                        fn_used_ac[tt4 % 2] = dv.n
                        ot_used_dv[tt4 % 2] = dv.n

                        if tt4 % 2 == 1:
                            def _(sync, kq=kq, tt4=tt4, w=dv.n):
                                sync.wait_ge(dv.h, w)
                                for db in range(NKD):
                                    ins = sync.dma_start(
                                        out=fin_hbm[db * 128:(db + 1) * 128,
                                                    kq * 512 + (tt4 - 1) * 128:
                                                    kq * 512 + (tt4 + 1) * 128],
                                        in_=fstage[:, db * 256:(db + 1) * 256])
                                    ld.inc(ins)
                            E["sync"](_)
                            fst_dma_ld[0] = ld.n



            yful_ld_of = [0] * 4
            yful_ld_of[0] = load_yful(0)
            yful_ld_of[1] = load_yful(1)
            for k in range(4):
                if k >= 1:
                    emit_chains(k - 1)
                yb = yful[k % 2]
                yful_ld = yful_ld_of[k]

                for mt in range(16):
                    wob = wopb[mt % 2]
                    def _(g, mt=mt, wob=wob, w=wop_dma_by_buf[mt % 2]):
                        if w:
                            g.wait_ge(pe.h, w)
                        gld.inc(wslice_dma(g, wob, ap_wop, mt, 4096))
                    E["gpsimd"](_)
                    w_gld = gld.n

                    bank_i = mt % 5
                    bank = pmm[bank_i]
                    def _(t, wob=wob, yb=yb, bank=bank, bank_i=bank_i,
                          w=w_gld, w2=yful_ld,
                          wb_=bank_free.get(f"o{bank_i}")):
                        t.wait_ge(gld.h, w)
                        t.wait_ge(yf.h, w2)
                        if wb_:
                            t.wait_ge(wb_[0].h, wb_[1])
                        for kt in range(32):
                            ins = t.matmul(
                                bank[:, 0:512],
                                wob[:, kt * 128:(kt + 1) * 128],
                                yb[:, kt * TC:(kt + 1) * TC],
                                start=(kt == 0), stop=(kt == 31))
                        pe.inc(ins)
                    E["tensor"](_)
                    wop_dma_by_buf[mt % 2] = pe.n

                    def _(v, bank=bank, mt=mt, w=pe.n):
                        v.wait_ge(pe.h, w)
                        dv.inc(v.tensor_copy(om_bf[:, mt * TC:(mt + 1) * TC],
                                             bank[:, 0:512]))
                    E["vector"](_)
                    bank_free[f"o{bank_i}"] = (dv, dv.n)
                yful_used_pe[k] = pe.n
                if k + 2 < 4:
                    yful_ld_of[k + 2] = load_yful(k + 2)
                ombf_dv = dv.n

                ombf_dv_of[k] = dv.n

            emit_chains(3)

            if stop == "O":
                def _(sync, w=ld.n):
                    sync.wait_ge(ld.h, w)
                    ld.inc(sync.dma_start(out=dbg["dbg_x1"], in_=x1_hbm))
                    ld.inc(sync.dma_start(out=dbg["dbg_fin"], in_=fin_hbm))
                    sync.wait_ge(ld.h, ld.n)
                E["sync"](_)
                def _(g, w=gld.n):
                    g.wait_ge(gld.h, w)
                E["gpsimd"](_)
                return nc

            barrier()

        with contextlib.ExitStack() as fctx:
            QW = 512
            finb = sb("finb", [128, NKD * QW], BF16, fctx)
            wgb = [sb(f"wgb{i}", [128, NKD * 128], BF16, fctx) for i in range(3)]
            wub = [sb(f"wub{i}", [128, NKD * 128], BF16, fctx) for i in range(3)]
            wdb = [sb(f"wdb{i}", [128, NHT * 128], BF16, fctx) for i in range(3)]
            gbuf2 = [sb(f"gbuf{i}", [128, QW], BF16, fctx) for i in range(2)]
            ubuf2 = [sb(f"ubuf{i}", [128, QW], BF16, fctx) for i in range(2)]
            gu_sb = sb("gu_sb", [128, NHT * QW], BF16, fctx)
            fo_bf = sb("fo_bf", [128, NKD * QW], BF16, fctx)
            fo_tm = [sb(f"fo_tm{i}", [128, D], F32, fctx) for i in range(2)]
            x1r = [sb(f"x1r{i}", [128, D], F32, fctx) for i in range(2)]
            fscr = sb("fscr", [128, D], F32, fctx)

            wq_dma_by_buf = {}
            fin_used_pe = 0
            gu_used_pe = 0
            x1r_used_dv = [0, 0]
            x2_st_by_buf = [0, 0]
            fo_used_dv = [0, 0]

            for hf in range(4):
                hcols = slice(hf * QW, (hf + 1) * QW)
                def _(sync, hcols=hcols, w=fin_used_pe):
                    if w:
                        sync.wait_ge(pe.h, w)
                    ld.inc(sync.dma_start(
                        out=finb[:].rearrange("p (dt c) -> p dt c", dt=NKD),
                        in_=fin_hbm[:, hcols].rearrange("(dt p) c -> p dt c", p=128)))
                E["sync"](_)
                fin_ld = ld.n

                gu_par_dv = [0, 0]
                gu_dma_gld = {}
                gu_sub_dv = {}

                def emit_gu_dma(ht):
                    def _(g, ht=ht, wgt=wgb[ht % 3], wut=wub[ht % 3],
                          w=wq_dma_by_buf.get(ht % 3, 0)):
                        if w:
                            g.wait_ge(pe.h, w)
                        gld.inc(wslice_dma(g, wgt, ap_wg, ht, 2048))
                        gld.inc(wslice_dma(g, wut, ap_wu, ht, 2048))
                    E["gpsimd"](_)
                    gu_dma_gld[ht] = gld.n

                def emit_gu_sub(ht):
                    def _(v, wgt=wgb[ht % 3], wut=wub[ht % 3],
                          w=gu_dma_gld[ht], wp=gu_used_pe):
                        v.wait_ge(gld.h, w)
                        if wp:
                            v.wait_ge(pe.h, wp)
                        v.tensor_scalar(wgt[:], wgt[:], 128.0, None, ALU.subtract)
                        dv.inc(v.tensor_scalar(wut[:], wut[:], 128.0, None,
                                               ALU.subtract))
                    E["vector"](_)
                    gu_sub_dv[ht] = dv.n

                emit_gu_dma(0)
                emit_gu_dma(1)
                emit_gu_sub(0)
                for ht in range(NHT):
                    wgt = wgb[ht % 3]
                    wut = wub[ht % 3]
                    if ht + 2 < NHT:
                        emit_gu_dma(ht + 2)
                    if ht + 1 < NHT:
                        emit_gu_sub(ht + 1)
                    wsub_dv = gu_sub_dv[ht]

                    for which, wt in enumerate((wgt, wut)):
                        bank_i = (ht * 2 + which) % 5
                        bank = pmm[bank_i]
                        def _(t, wt=wt, bank=bank, bank_i=bank_i,
                              w=wsub_dv, w2=fin_ld,
                              wb_=bank_free.get(f"f{bank_i}")):
                            t.wait_ge(dv.h, w)
                            t.wait_ge(ld.h, w2)
                            if wb_:
                                t.wait_ge(wb_[0].h, wb_[1])
                            for kt in range(NKD):
                                ins = t.matmul(
                                    bank[:, 0:QW],
                                    wt[:, kt * 128:(kt + 1) * 128],
                                    finb[:, kt * QW:(kt + 1) * QW],
                                    start=(kt == 0), stop=(kt == NKD - 1))
                            pe.inc(ins)
                        E["tensor"](_)
                        if which == 0:
                            def _(s, bank=bank, ht=ht, w=pe.n,
                                  wg_=gu_par_dv[ht % 2]):
                                s.wait_ge(pe.h, w)
                                if wg_:
                                    s.wait_ge(dv.h, wg_)
                                ac.inc(s.activation(
                                    gbuf2[ht % 2][:], bank[:, 0:QW], AF.Silu,
                                    bias=0.0, scale=sg_sb[:, ht:ht + 1]))
                            E["scalar"](_)
                            bank_free[f"f{bank_i}"] = (ac, ac.n)
                        else:
                            def _(v, bank=bank, ht=ht, w=pe.n):
                                v.wait_ge(pe.h, w)
                                dv.inc(v.tensor_scalar_mul(
                                    ubuf2[ht % 2][:], bank[:, 0:QW],
                                    su_sb[:, ht:ht + 1]))
                            E["vector"](_)
                            bank_free[f"f{bank_i}"] = (dv, dv.n)

                    wq_dma_by_buf[ht % 3] = pe.n
                    def _(v, ht=ht, w=ac.n):
                        v.wait_ge(ac.h, w)
                        dv.inc(v.tensor_tensor(
                            gu_sb[:, ht * QW:(ht + 1) * QW],
                            gbuf2[ht % 2][:], ubuf2[ht % 2][:], ALU.mult))
                    E["vector"](_)
                    gu_par_dv[ht % 2] = dv.n
                gu_dv = dv.n
                fin_used_pe = pe.n

                wd_dma_gld = {}
                wd_sub_dv = {}

                def emit_wd_dma(mt):
                    def _(g, mt=mt, wdt=wdb[mt % 3],
                          w=wq_dma_by_buf.get(f"d{mt % 3}", 0)):
                        if w:
                            g.wait_ge(pe.h, w)
                        gld.inc(wslice_dma(g, wdt, ap_wd, mt, 5632))
                    E["gpsimd"](_)
                    wd_dma_gld[mt] = gld.n

                def emit_wd_sub(mt):
                    def _(v, wdt=wdb[mt % 3], w=wd_dma_gld[mt], wp=gu_used_pe):
                        v.wait_ge(gld.h, w)
                        if wp:
                            v.wait_ge(pe.h, wp)
                        dv.inc(v.tensor_scalar(wdt[:], wdt[:], 128.0, None,
                                               ALU.subtract))
                    E["vector"](_)
                    wd_sub_dv[mt] = dv.n

                emit_wd_dma(0)
                emit_wd_dma(1)
                emit_wd_sub(0)
                for mt in range(16):
                    wdt = wdb[mt % 3]
                    if mt + 2 < 16:
                        emit_wd_dma(mt + 2)
                    if mt + 1 < 16:
                        emit_wd_sub(mt + 1)
                    wd_dv = wd_sub_dv[mt]

                    bank_i = mt % 5
                    bank = pmm[bank_i]
                    def _(t, wdt=wdt, bank=bank, bank_i=bank_i,
                          w=wd_dv, w2=gu_dv,
                          wb_=bank_free.get(f"f{bank_i}")):
                        t.wait_ge(dv.h, w)
                        t.wait_ge(dv.h, w2)
                        if wb_:
                            t.wait_ge(wb_[0].h, wb_[1])
                        for kt in range(NHT):
                            ins = t.matmul(
                                bank[:, 0:QW],
                                wdt[:, kt * 128:(kt + 1) * 128],
                                gu_sb[:, kt * QW:(kt + 1) * QW],
                                start=(kt == 0), stop=(kt == NHT - 1))
                        pe.inc(ins)
                    E["tensor"](_)
                    wq_dma_by_buf[f"d{mt % 3}"] = pe.n
                    def _(v, bank=bank, mt=mt, w=pe.n):
                        v.wait_ge(pe.h, w)
                        dv.inc(v.tensor_scalar_mul(
                            fo_bf[:, mt * QW:(mt + 1) * QW],
                            bank[:, 0:QW], sd_sb[:, mt:mt + 1]))
                    E["vector"](_)
                    bank_free[f"f{bank_i}"] = (dv, dv.n)
                gu_used_pe = pe.n
                fo_dv = dv.n

                if stop == "F":
                    def _(sync, hf=hf, w=gu_dv, w2=fo_dv):
                        sync.wait_ge(dv.h, w)
                        sync.wait_ge(dv.h, w2)
                        for kt in range(NHT):
                            ins = sync.dma_start(
                                out=dbg["dbg_gu"][kt * 128:(kt + 1) * 128,
                                                  hf * QW:(hf + 1) * QW],
                                in_=gu_sb[:, kt * QW:(kt + 1) * QW])
                            ld.inc(ins)
                        for mt in range(16):
                            ins = sync.dma_start(
                                out=dbg["dbg_fo"][mt * 128:(mt + 1) * 128,
                                                  hf * QW:(hf + 1) * QW],
                                in_=fo_bf[:, mt * QW:(mt + 1) * QW])
                            ld.inc(ins)
                    E["sync"](_)

                for tt8 in range(4):
                    myrow = hf * QW + tt8 * 128
                    fot = fo_tm[tt8 % 2]
                    x1t = x1r[tt8 % 2]

                    def _(sync, x1t=x1t, myrow=myrow, w=x1r_used_dv[tt8 % 2],
                          ws=x2_st_by_buf[tt8 % 2]):
                        if w:
                            sync.wait_ge(dv.h, w)
                        if ws:
                            sync.wait_ge(st.h, ws)
                        ld.inc(sync.dma_start(
                            out=x1t[:], in_=x1_hbm[myrow:myrow + 128, :]))
                    E["sync"](_)
                    x1_ld = ld.n

                    for grp in range(2):
                        pb = ptb[grp]
                        def _(t, pb=pb, grp=grp, tt8=tt8, w=fo_dv,
                              w2=fo_used_dv[tt8 % 2],
                              w3=ptb_evac_global[grp]):
                            t.wait_ge(dv.h, w)
                            if w2:
                                t.wait_ge(dv.h, w2)
                            if w3:
                                t.wait_ge(dv.h, w3)
                            for blk in range(8):
                                mt = grp * 8 + blk
                                ins = t.transpose(
                                    pb[:, blk * 128:(blk + 1) * 128],
                                    fo_bf[:, mt * QW + tt8 * 128:
                                          mt * QW + tt8 * 128 + 128],
                                    idbf_sb[:])
                            pe.inc(ins)
                        E["tensor"](_)
                        def _(v, pb=pb, grp=grp, fot=fot, w=pe.n):
                            v.wait_ge(pe.h, w)
                            src = pb[:].rearrange("p (b c) -> p b c", b=8)
                            dst = fot[:].rearrange("p (mt c) -> p mt c", mt=16)[
                                :, grp * 8:(grp + 1) * 8, :]
                            dv.inc(v.tensor_copy(dst, src))
                        E["vector"](_)
                        ptb_evac_global[grp] = dv.n

                    def _(v, fot=fot, x1t=x1t, wl=x1_ld,
                          ws=x2_st_by_buf[tt8 % 2]):
                        v.wait_ge(ld.h, wl)
                        if ws:
                            v.wait_ge(st.h, ws)
                        v.tensor_scalar(fscr[:], fot[:], 1.0, 0.0, ALU.mult,
                                        ALU.add, accum_out=ost[:, 12:13])
                        v.scalar_tensor_tensor(fscr[:], fot[:], 1.0, fot[:],
                                               ALU.mult, ALU.mult,
                                               accum_out=ost[:, 13:14])
                        v.tensor_scalar_mul(ost[:, 12:13], ost[:, 12:13], 1.0 / D)
                        v.tensor_scalar_mul(ost[:, 13:14], ost[:, 13:14], 1.0 / D)
                        v.tensor_tensor(ost[:, 14:15], ost[:, 12:13],
                                        ost[:, 12:13], ALU.mult)
                        v.tensor_tensor(ost[:, 13:14], ost[:, 13:14],
                                        ost[:, 14:15], ALU.subtract)
                        dv.inc(v.tensor_scalar(ost[:, 13:14], ost[:, 13:14],
                                               1.0, 1e-5, ALU.mult, ALU.add))
                    E["vector"](_)
                    def _(s, w=dv.n):
                        s.wait_ge(dv.h, w)
                        ac.inc(s.activation(ost[:, 13:14], ost[:, 13:14],
                                            AF.Sqrt, bias=0.0, scale=1.0))
                    E["scalar"](_)
                    def _(v, fot=fot, x1t=x1t, w=ac.n):
                        v.wait_ge(ac.h, w)
                        v.reciprocal(ost[:, 13:14], ost[:, 13:14])
                        v.tensor_scalar(fscr[:], fot[:], ost[:, 12:13],
                                        ost[:, 13:14], ALU.subtract, ALU.mult)
                        dv.inc(v.tensor_tensor(x1t[:], fscr[:], x1t[:], ALU.add))
                    E["vector"](_)
                    x1r_used_dv[tt8 % 2] = dv.n
                    fo_used_dv[tt8 % 2] = dv.n

                    def _(sync, x1t=x1t, myrow=myrow, w=dv.n):
                        sync.wait_ge(dv.h, w)
                        st.inc(sync.dma_start(
                            out=ap_o[myrow:myrow + 128, :], in_=x1t[:]))
                    E["sync"](_)
                    x2_st_by_buf[tt8 % 2] = st.n

            def _(sync, w=st.n, w2=ld.n):
                sync.wait_ge(st.h, w)
                sync.wait_ge(ld.h, w2)
            E["sync"](_)
            def _(g, w=gld.n):
                g.wait_ge(gld.h, w)
            E["gpsimd"](_)

    return nc


_NC_CACHE = {}


def kernel(**inputs):
    d = {k: np.asarray(v) for k, v in inputs.items()}
    idbf = np.eye(128).astype(BF)

    def pcols(a, width=128):
        a = np.asarray(a).astype(np.float32)
        k = a.shape[0] // width
        rest = int(np.prod(a.shape[1:])) if a.ndim > 1 else 1
        return np.ascontiguousarray(
            a.reshape(k, width, rest).transpose(1, 0, 2).reshape(width, k * rest))

    def wpack(a, nkt, nmt):
        a = np.asarray(a)
        return np.ascontiguousarray(
            a.reshape(nkt, 128, nmt, 128).transpose(1, 2, 0, 3)
            .reshape(128, nmt * nkt * 128))

    wip_full = d["L0_attn_in_proj_w"].astype(np.float32)
    wxp = d["L0_attn_x_proj_w"].astype(np.float32)
    wop = wpack(np.ascontiguousarray(d["L0_attn_out_proj_w"].astype(np.float32).T),
                32, 16)
    wg = wpack(np.ascontiguousarray(d["L0_ffn_wq_gate"].T).astype(np.uint8), 16, NHT)
    wu = wpack(np.ascontiguousarray(d["L0_ffn_wq_up"].T).astype(np.uint8), 16, NHT)
    wd = wpack(np.ascontiguousarray(d["L0_ffn_wq_down"].T).astype(np.uint8), NHT, 16)
    x = d["L0_attn_x"].astype(np.float32)

    maps = []
    for cid in range(8):
        b, dh = cid // 2, cid % 2
        sl = slice(dh * DIH, (dh + 1) * DIH)
        wip = wpack(np.ascontiguousarray(
            np.concatenate([wip_full[sl],
                            wip_full[DI + dh * DIH:DI + (dh + 1) * DIH]], 0).T),
            16, 32)
        wxpc = np.ascontiguousarray(wxp[:, sl].T)
        wxpc = np.ascontiguousarray(
            wxpc.reshape(16, 128, 33).transpose(1, 0, 2).reshape(128, 16 * 33))
        maps.append(dict(
            x=np.ascontiguousarray(x[b]),
            xmy=np.ascontiguousarray(x[b, dh * MYT:(dh + 1) * MYT]),
            wip=wip,
            wxp=wxpc,
            wop=wop,
            convw=pcols(d["L0_attn_conv_w"][sl, 0, :]),
            convb=pcols(d["L0_attn_conv_b"][sl]),
            dtw=pcols(d["L0_attn_dt_w"][sl, 0]),
            dtb=pcols(d["L0_attn_dt_b"][sl]),
            alog=pcols(d["L0_attn_A_log"][sl]),
            dpar=pcols(d["L0_attn_D_param"][sl]),
            nw=pcols(d["L0_attn_norm_w"]),
            wg=wg, wu=wu, wd=wd,
            sg=pcols(d["L0_ffn_s_gate"]),
            su=pcols(d["L0_ffn_s_up"]),
            sd=pcols(d["L0_ffn_s_down"]),
            idbf=idbf,
        ))

    if "nc" not in _NC_CACHE:
        _NC_CACHE["nc"] = build("full")
    nc = _NC_CACHE["nc"]
    res = run_bass_kernel_spmd(nc, maps, core_ids=list(range(8)))
    out = np.empty((B, T, D), np.float32)
    for cid in range(8):
        b, dh = cid // 2, cid % 2
        out[b, dh * MYT:(dh + 1) * MYT] = res.results[cid]["o"]
    return out
